# version 10
# baseline (speedup 1.0000x reference)
import numpy as np
import concourse.bass as bass
import concourse.mybir as mybir
from concourse.bass_utils import run_bass_kernel_spmd

F32 = mybir.dt.float32
BF16 = mybir.dt.bfloat16
AF = mybir.ActivationFunctionType
OP = mybir.AluOpType
AX = mybir.AxisListType

P = 128
T = 128
NCH = 32
T2 = 256
NCH2 = 16
DM = 1024
DMT = 8
DIN = 2048
DD = 16
NST = 16
RNK = 64
L = 4096


def PS(lo, hi):
    return [("ps", u) for u in range(lo // 512, (hi + 511) // 512)]


class Prog:
    def __init__(self):
        self.ops = []
        self.seq = {}
        self.cnt = {}

    def add(self, eng, emit, deps=(), sem=None):
        sem = sem or eng
        self.cnt[sem] = self.cnt.get(sem, 0) + 1
        i = len(self.ops)
        self.ops.append((eng, emit, tuple(d for d in deps if d is not None)))
        self.seq[i] = (sem, self.cnt[sem])
        return i

    def emit_for(self, ename, eng, sems, mult):
        waited = {}
        for i, (e, emitfn, deps) in enumerate(self.ops):
            if e != ename:
                continue
            mysem = self.seq[i][0]
            for d in deps:
                dsem, dcnt = self.seq[d]
                tgt = dcnt * mult[dsem]
                if waited.get(dsem, -1) < tgt:
                    eng.wait_ge(sems[dsem], tgt)
                    waited[dsem] = tgt
            last = emitfn(eng)
            last.then_inc(sems[mysem], mult[mysem])


NLANES_L = 8
NLANES_S = 4


class Tr:
    def __init__(self, prog):
        self.p = prog
        self.w = {}
        self.r = {}
        self.lane_idx = {"qL": 0, "qS": 0}
        self.lane_last = {}

    def op(self, eng, emit, reads=(), writes=(), deps=(), sem=None):
        dl = list(deps)
        for k in reads:
            if k in self.w:
                dl.append(self.w[k])
        for k in writes:
            dl += self.r.get(k, [])
            if k in self.w:
                dl.append(self.w[k])
        if sem in ("qL", "qS"):
            nl = NLANES_L if sem == "qL" else NLANES_S
            li = self.lane_idx[sem] % nl
            self.lane_idx[sem] += 1
            lane = f"{sem}{li}"
            if lane in self.lane_last:
                dl.append(self.lane_last[lane])
            i = self.p.add(eng, emit, dl, sem=lane)
            self.lane_last[lane] = i
        else:
            i = self.p.add(eng, emit, dl, sem=sem)
        for k in reads:
            self.r.setdefault(k, []).append(i)
        for k in writes:
            self.w[k] = i
            self.r[k] = []
        return i


def build(n_mix=NCH, n_f1=NCH2, n_f2=NCH2, max_ops=None):
    nc = bass.Bass("TRN2", target_bir_lowering=False)

    d_x = nc.dram_tensor("x", [L, DM], F32, kind="ExternalInput").ap()
    d_inw = nc.dram_tensor("inw", [2 * DIN, DM], F32, kind="ExternalInput").ap()
    d_cw = nc.dram_tensor("cw", [DIN, 4], F32, kind="ExternalInput").ap()
    d_cb = nc.dram_tensor("cb", [DIN, 1], F32, kind="ExternalInput").ap()
    d_xpw = nc.dram_tensor("xpw", [96, DIN], F32, kind="ExternalInput").ap()
    d_dtw = nc.dram_tensor("dtw", [DIN, RNK], F32, kind="ExternalInput").ap()
    d_dtb = nc.dram_tensor("dtb", [DIN, 1], F32, kind="ExternalInput").ap()
    d_alog = nc.dram_tensor("alog", [DIN, NST], F32, kind="ExternalInput").ap()
    d_dv = nc.dram_tensor("dv", [DIN, 1], F32, kind="ExternalInput").ap()
    d_outw = nc.dram_tensor("outw", [DM, DIN], F32, kind="ExternalInput").ap()
    d_w1 = nc.dram_tensor("w1", [4096, DM], F32, kind="ExternalInput").ap()
    d_b1 = nc.dram_tensor("b1", [4096, 1], F32, kind="ExternalInput").ap()
    d_w2 = nc.dram_tensor("w2", [DM, 4096], F32, kind="ExternalInput").ap()
    d_b2 = nc.dram_tensor("b2", [1, DM], F32, kind="ExternalInput").ap()
    d_out = nc.dram_tensor("out", [L, DM], F32, kind="ExternalOutput").ap()
    d_x1s = nc.dram_tensor("x1s", [L, DM], F32, kind="Internal").ap()
    d_h1s = nc.dram_tensor("h1s", [4096, L], BF16, kind="Internal").ap()

    wbA = nc.alloc_sbuf_tensor("wbA", [P, 32768], BF16)
    wbB = nc.alloc_sbuf_tensor("wbB", [P, 16384], BF16)
    wf = nc.alloc_sbuf_tensor("wf", [P, 3584], F32)
    af = nc.alloc_sbuf_tensor("af", [P, 21504], F32)
    ab = nc.alloc_sbuf_tensor("ab", [P, 5120], BF16)
    sm = nc.alloc_sbuf_tensor("sm", [P, 512], F32)
    onesf = nc.alloc_sbuf_tensor("onesf", [1, P], F32)
    onesb = nc.alloc_sbuf_tensor("onesb", [1, P], BF16)
    idf = nc.alloc_sbuf_tensor("idf", [P, P], F32)
    idb = nc.alloc_sbuf_tensor("idb", [P, P], BF16)

    b2b = ab[0:1, 4096:4096 + DM]
    ppf = nc.alloc_psum_tensor("ppf", [P, 3584], F32)
    ppb = nc.alloc_psum_tensor("ppb", [P, 1024], BF16)

    convw = sm[:, 0:64]
    convb = sm[:, 64:80]
    dtbv = sm[:, 80:96]
    Dv = sm[:, 96:112]
    alpha1 = sm[:, 112:144]
    b1s = sm[:, 144:176]
    alpha2q = sm[:, 176:208]
    alpha2 = sm[:, 208:216]
    stats = sm[:, 216:240]
    Aneg = sm[:, 240:496]
    eps1 = sm[:, 496:497]
    nc.const_aps.register(F32, 1e-5, eps1[:])

    X0, X1 = 0, 1024
    XSP = 2048
    XSC = 4144
    DT = 6192
    ABUF = 8240
    BBUF = 10288
    HBUF = 12336
    YDD = 14384
    UDD = 14512
    FIXA = 14640
    FIXB = 14656
    INIT = 14672
    DTR = 14928
    BCS = 15056
    CEX = 15184
    AB_ = [8240, 15696]
    BB_ = [10288, 17744]
    YD_ = [14384, 19792]
    UD_ = [14512, 19920]
    FA_ = [14640, 20048]
    FB_ = [14656, 20064]
    X1O = 20480
    FX1 = 0
    FOUT = 2048
    FSLAB = 4096
    FSGN = 5120
    XT = 0
    SVZ = 1024
    GTD = 3072
    FX1T = 0
    FH1 = 2048

    sems = {}
    sem_names = (["te", "sc", "ve", "gp"] +
                 [f"qL{i}" for i in range(NLANES_L)] +
                 [f"qS{i}" for i in range(NLANES_S)])
    for s in sem_names:
        sems[s] = nc.alloc_semaphore(s)
    mult = {s: (16 if s.startswith("q") else 1) for s in sem_names}

    prog = Prog()
    tr = Tr(prog)

    def barrier_anchor():
        last = {}
        for i in range(len(prog.ops)):
            last[prog.seq[i][0]] = i
        return list(last.values())

    tr.op("gp", lambda g: g.memset(idf[:], 0.0), writes=["idf"])
    tr.op("gp", lambda g: g.affine_select(
        out=idf[:], in_=idf[:], compare_op=OP.not_equal, fill=1.0,
        base=0, pattern=[[-1, P]], channel_multiplier=1),
        reads=["idf"], writes=["idf"])
    tr.op("gp", lambda g: g.memset(idb[:], 0.0), writes=["idb"])
    tr.op("gp", lambda g: g.affine_select(
        out=idb[:], in_=idb[:], compare_op=OP.not_equal, fill=1.0,
        base=0, pattern=[[-1, P]], channel_multiplier=1),
        reads=["idb"], writes=["idb"])
    tr.op("gp", lambda g: g.memset(onesf[:], 1.0), writes=["onesf"])
    tr.op("gp", lambda g: g.memset(onesb[:], 1.0), writes=["onesb"])
    tr.op("gp", lambda g: g.memset(eps1[:], 1e-5), writes=["st"])
    tr.op("ve", lambda v: v.memset(af[:, XSP:XSP + 2096], 0.0),
          writes=[("xsp", dd) for dd in range(DD)])
    tr.op("ve", lambda v: v.memset(af[:, INIT:INIT + 256], 0.0),
          writes=[("init", dd) for dd in range(DD)])

    for dd in range(DD):
        tr.op("sync", lambda s, dd=dd: s.dma_start(
            out=convw[:, dd * 4:dd * 4 + 4], in_=d_cw[dd * P:(dd + 1) * P, :]),
            writes=[("convw", dd)], sem="qL")
        tr.op("sync", lambda s, dd=dd: s.dma_start(
            out=convb[:, dd:dd + 1], in_=d_cb[dd * P:(dd + 1) * P, :]),
            writes=[("convb", dd)], sem="qL")
        tr.op("sync", lambda s, dd=dd: s.dma_start(
            out=dtbv[:, dd:dd + 1], in_=d_dtb[dd * P:(dd + 1) * P, :]),
            writes=[("dtbv", dd)], sem="qL")
        tr.op("sync", lambda s, dd=dd: s.dma_start(
            out=Dv[:, dd:dd + 1], in_=d_dv[dd * P:(dd + 1) * P, :]),
            writes=[("Dv", dd)], sem="qL")
        tr.op("sync", lambda s, dd=dd: s.dma_start(
            out=Aneg[:, dd * 16:dd * 16 + 16],
            in_=d_alog[dd * P:(dd + 1) * P, :]),
            writes=[("An", dd)], sem="qL")
    allA = [("An", dd) for dd in range(DD)]
    tr.op("sc", lambda c: c.activation(out=Aneg[:], in_=Aneg[:], func=AF.Exp),
          reads=allA, writes=allA)
    tr.op("ve", lambda v: v.tensor_scalar(
        out=Aneg[:], in0=Aneg[:], scalar1=-1.0, scalar2=None, op0=OP.mult),
        reads=allA, writes=allA)

    for s in range(32):
        tr.op("sync", lambda q, s=s: q.dma_start(
            out=af[:, 0:1024], in_=d_inw[s * P:(s + 1) * P, :]),
            writes=["pslab"], sem="qL")
        tr.op("sc", lambda c: c.copy(out=ab[:, 0:1024], in_=af[:, 0:1024]),
              reads=["pslab"], writes=["bslab"])
        for kt in range(DMT):
            tr.op("te", lambda t, kt=kt: t.transpose(
                out=ppb[:, kt * P:(kt + 1) * P],
                in_=ab[:, kt * P:(kt + 1) * P], identity=idb[:]),
                reads=["bslab", "idb"], writes=[("pb", kt), "pbB"])
        for kt in range(DMT):
            tr.op("sc", lambda c, s=s, kt=kt: c.copy(
                out=wbA[:, (kt * 32 + s) * P:(kt * 32 + s + 1) * P],
                in_=ppb[:, kt * P:(kt + 1) * P]),
                reads=[("pb", kt), "pbB"], writes=[("wA", kt * 32 + s)])

    for s in range(8):
        tr.op("sync", lambda q, s=s: q.dma_start(
            out=af[:, 0:2048], in_=d_outw[s * P:(s + 1) * P, :]),
            writes=["pslab"], sem="qL")
        tr.op("sc", lambda c: c.copy(out=ab[:, 0:2048], in_=af[:, 0:2048]),
              reads=["pslab"], writes=["bslab"])
        for half in range(2):
            for j in range(8):
                kt = half * 8 + j
                tr.op("te", lambda t, kt=kt, j=j: t.transpose(
                    out=ppb[:, j * P:(j + 1) * P],
                    in_=ab[:, kt * P:(kt + 1) * P], identity=idb[:]),
                    reads=["bslab", "idb"], writes=[("pb", j), "pbB"])
            for j in range(8):
                kt = half * 8 + j
                tr.op("sc", lambda c, s=s, kt=kt, j=j: c.copy(
                    out=wbB[:, kt * 1024 + s * P:kt * 1024 + (s + 1) * P],
                    in_=ppb[:, j * P:(j + 1) * P]),
                    reads=[("pb", j), "pbB"], writes=[("wB", kt)])

    tr.op("sync", lambda q: q.dma_start(out=af[0:96, 0:2048], in_=d_xpw),
          writes=["pslab"], sem="qL")
    for kt in range(DD):
        lo = (kt % 2) * 512
        tr.op("te", lambda t, kt=kt, lo=lo: t.transpose(
            out=ppf[:, lo:lo + 96],
            in_=af[0:96, kt * P:(kt + 1) * P], identity=idf[0:96, 0:96]),
            reads=["pslab", "idf"], writes=PS(lo, lo + 512))
        tr.op("sc", lambda c, kt=kt, lo=lo: c.copy(
            out=wf[:, kt * 96:(kt + 1) * 96], in_=ppf[:, lo:lo + 96]),
            reads=PS(lo, lo + 512), writes=[("wxp", kt)])

    for dd in range(DD):
        lo = (dd % 2) * 512
        tr.op("sync", lambda q, dd=dd: q.dma_start(
            out=af[:, 0:64], in_=d_dtw[dd * P:(dd + 1) * P, :]),
            writes=["pslab"], sem="qL")
        tr.op("te", lambda t, dd=dd, lo=lo: t.transpose(
            out=ppf[0:64, lo:lo + P], in_=af[:, 0:64], identity=idf[:]),
            reads=["pslab", "idf"], writes=PS(lo, lo + 512))
        tr.op("sc", lambda c, dd=dd, lo=lo: c.copy(
            out=wf[0:64, 1536 + dd * P:1536 + (dd + 1) * P],
            in_=ppf[0:64, lo:lo + P]),
            reads=PS(lo, lo + 512), writes=[("wdt", dd)])

    def xs_ap(dd, lo, hi):
        return af[:, XSP + dd * 131 + lo:XSP + dd * 131 + hi]

    BSB = HBUF

    def loadx(c):
        t0 = c * T
        xo = X0 if (c % 2 == 0) else X1
        xkey = ("xslot", c % 2)
        tr.op("sync", lambda q, t0=t0, xo=xo: q.dma_start(
            out=af[:, xo:xo + 1024], in_=d_x[t0:t0 + P, :]),
            writes=[xkey], deps=pass1_anchor if c == 0 else (), sem="qL")

    def front_A(c):
        xo = X0 if (c % 2 == 0) else X1
        xkey = ("xslot", c % 2)
        for kt in range(DMT):
            tr.op("te", lambda t, kt=kt, xo=xo: t.transpose(
                out=ppf[:, kt * P:(kt + 1) * P],
                in_=af[:, xo + kt * P:xo + (kt + 1) * P], identity=idf[:]),
                reads=[xkey, "idf"], writes=PS(kt * P, (kt + 1) * P))
        for kt in range(DMT):
            tr.op("sc", lambda c_, kt=kt: c_.copy(
                out=ab[:, XT + kt * P:XT + (kt + 1) * P],
                in_=ppf[:, kt * P:(kt + 1) * P]),
                reads=PS(kt * P, (kt + 1) * P), writes=[("xT", kt)])

    def inproj_pair(c, i):
        for mt in (2 * i, 2 * i + 1):
            po = 1024 + (mt % 2) * 512
            for kt in range(DMT):
                tr.op("te", lambda t, mt=mt, kt=kt, po=po: t.matmul(
                    out=ppf[:, po:po + P],
                    lhsT=wbA[:, (kt * 32 + mt) * P:(kt * 32 + mt + 1) * P],
                    rhs=ab[:, XT + kt * P:XT + (kt + 1) * P],
                    start=(kt == 0), stop=(kt == DMT - 1)),
                    reads=[("xT", kt), ("wA", kt * 32 + mt)],
                    writes=PS(po, po + P))
            if mt < DD:
                tr.op("sc", lambda c_, mt=mt, po=po: c_.copy(
                    out=xs_ap(mt, 3, 131), in_=ppf[:, po:po + P]),
                    reads=PS(po, po + P), writes=[("xsp", mt)])
            else:
                tr.op("sc", lambda c_, mt=mt, po=po: c_.activation(
                    out=ab[:, SVZ + (mt - DD) * T:SVZ + (mt - DD + 1) * T],
                    in_=ppf[:, po:po + P], func=AF.Silu),
                    reads=PS(po, po + P), writes=[("svz", mt - DD)])

    def conv_pair(c, j):
        for dd in (2 * j, 2 * j + 1):
            pr = dd % 2
            UO, YO = UD_[pr], YD_[pr]
            tr.op("ve", lambda v, dd=dd, UO=UO: v.tensor_scalar(
                out=af[:, UO:UO + T], in0=xs_ap(dd, 0, T),
                scalar1=convw[:, dd * 4:dd * 4 + 1], scalar2=None,
                op0=OP.mult),
                reads=[("xsp", dd), ("convw", dd)], writes=[("u", pr)])
            tr.op("ve", lambda v, dd=dd, UO=UO, YO=YO: v.scalar_tensor_tensor(
                out=af[:, YO:YO + T], in0=xs_ap(dd, 1, 1 + T),
                scalar=convw[:, dd * 4 + 1:dd * 4 + 2],
                in1=af[:, UO:UO + T], op0=OP.mult, op1=OP.add),
                reads=[("xsp", dd), ("u", pr), ("convw", dd)],
                writes=[("y", pr)])
            tr.op("ve", lambda v, dd=dd, UO=UO, YO=YO: v.scalar_tensor_tensor(
                out=af[:, UO:UO + T], in0=xs_ap(dd, 2, 2 + T),
                scalar=convw[:, dd * 4 + 2:dd * 4 + 3],
                in1=af[:, YO:YO + T], op0=OP.mult, op1=OP.add),
                reads=[("xsp", dd), ("y", pr), ("convw", dd)],
                writes=[("u", pr)])
            tr.op("ve", lambda v, dd=dd, UO=UO, YO=YO: v.scalar_tensor_tensor(
                out=af[:, YO:YO + T], in0=xs_ap(dd, 3, 3 + T),
                scalar=convw[:, dd * 4 + 3:dd * 4 + 4],
                in1=af[:, UO:UO + T], op0=OP.mult, op1=OP.add),
                reads=[("xsp", dd), ("u", pr), ("convw", dd)],
                writes=[("y", pr)])
            tr.op("sc", lambda c_, dd=dd, YO=YO: c_.activation(
                out=af[:, XSC + dd * T:XSC + (dd + 1) * T],
                in_=af[:, YO:YO + T], func=AF.Silu,
                bias=convb[:, dd:dd + 1]),
                reads=[("y", pr), ("convb", dd)], writes=[("xsc", dd)])
            tr.op("ve", lambda v, dd=dd: v.tensor_copy(
                out=xs_ap(dd, 0, 3), in_=xs_ap(dd, 128, 131)),
                reads=[("xsp", dd)], writes=[("xsp", dd)])

    def emit_agen(c, dd):
        pr = dd % 2
        AO = AB_[pr]
        for n in range(NST):
            tr.op("sc", lambda c_, dd=dd, n=n, AO=AO: c_.activation(
                out=af[:, AO + n * T:AO + (n + 1) * T],
                in_=af[:, DT + dd * T:DT + (dd + 1) * T], func=AF.Exp,
                scale=Aneg[:, dd * 16 + n:dd * 16 + n + 1]),
                reads=[("dt", dd), ("An", dd)], writes=[("a", pr, n)])

    def front_B(c):
        for kt in range(DD):
            tr.op("te", lambda t, kt=kt: t.matmul(
                out=ppf[0:96, 0:P],
                lhsT=wf[:, kt * 96:(kt + 1) * 96],
                rhs=af[:, XSC + kt * T:XSC + (kt + 1) * T],
                start=(kt == 0), stop=(kt == DD - 1)),
                reads=[("xsc", kt), ("wxp", kt)], writes=PS(0, 512))
        tr.op("sc", lambda c_: c_.copy(out=af[0:64, DTR:DTR + P],
                                       in_=ppf[0:64, 0:P]),
              reads=PS(0, 512), writes=["dtr"])
        tr.op("sc", lambda c_: c_.copy(out=af[0:32, BCS:BCS + P],
                                       in_=ppf[64:96, 0:P]),
              reads=PS(0, 512), writes=["bcs"])
        for dd in range(DD):
            po = (dd % 2) * 512
            spo = YD_[dd % 2]
            tr.op("te", lambda t, dd=dd, po=po: t.matmul(
                out=ppf[:, po:po + P],
                lhsT=wf[0:64, 1536 + dd * P:1536 + (dd + 1) * P],
                rhs=af[0:64, DTR:DTR + P], start=True, stop=True),
                reads=["dtr", ("wdt", dd)], writes=PS(po, po + 512))
            tr.op("ve", lambda v, dd=dd, po=po: v.tensor_scalar(
                out=af[:, DT + dd * T:DT + (dd + 1) * T],
                in0=ppf[:, po:po + P], scalar1=dtbv[:, dd:dd + 1],
                scalar2=80.0, op0=OP.add, op1=OP.min),
                reads=PS(po, po + 512) + [("dtbv", dd)], writes=[("dt", dd)])
            tr.op("sc", lambda c_, dd=dd, spo=spo: c_.activation(
                out=af[:, spo:spo + T],
                in_=af[:, DT + dd * T:DT + (dd + 1) * T], func=AF.Exp),
                reads=[("dt", dd)], writes=[("y", dd % 2)])
            tr.op("sc", lambda c_, dd=dd, spo=spo: c_.activation(
                out=af[:, DT + dd * T:DT + (dd + 1) * T],
                in_=af[:, spo:spo + T], func=AF.Ln, bias=1.0),
                reads=[("y", dd % 2)], writes=[("dt", dd)])
            if dd == 0:
                emit_agen(c, 0)

    def sel(row):
        return idf[0:32, row:row + 1].broadcast_to([32, P])

    def bcastB(c):
        for r in range(4):
            lo = (r % 2) * 512
            for j in range(4):
                n = r * 4 + j
                tr.op("te", lambda t, n=n, lo=lo, j=j: t.matmul(
                    out=ppf[:, lo + j * P:lo + (j + 1) * P], lhsT=sel(n),
                    rhs=af[0:32, BCS:BCS + P], start=True, stop=True),
                    reads=["bcs", "idf"], writes=PS(lo, lo + 512))
            tr.op("sc", lambda c_, r=r, lo=lo: c_.copy(
                out=af[:, BSB + r * 512:BSB + (r + 1) * 512],
                in_=ppf[:, lo:lo + 512]),
                reads=PS(lo, lo + 512), writes=[("bsb", r)])

    def bcastC(c):
        for j in range(4):
            lo = 2048 + j * P
            tr.op("te", lambda t, j=j, lo=lo: t.matmul(
                out=ppf[:, lo:lo + P], lhsT=sel(28 + j),
                rhs=af[0:32, BCS:BCS + P], start=True, stop=True),
                reads=["bcs", "idf"], writes=PS(2048, 2560))
        tr.op("sc", lambda c_: c_.copy(out=af[:, CEX:CEX + 512],
                                       in_=ppf[:, 2048:2560]),
              reads=PS(2048, 2560), writes=["cex"])
        for n in range(12):
            lo = 2048 + n * P
            tr.op("te", lambda t, n=n, lo=lo: t.matmul(
                out=ppf[:, lo:lo + P], lhsT=sel(16 + n),
                rhs=af[0:32, BCS:BCS + P], start=True, stop=True),
                reads=["bcs", "idf"], writes=PS(lo, lo + P))

    def scan_dd(c, dd):
        pr = dd % 2
        AO, BO, YO, UO = AB_[pr], BB_[pr], YD_[pr], UD_[pr]
        HO = BB_[1 - pr]
        FAo, FBo = FA_[pr], FB_[pr]
        a_nt = af[:, AO:AO + 2048].rearrange("p (n t) -> p n t", n=NST)
        b_nt = af[:, BO:BO + 2048].rearrange("p (n t) -> p n t", n=NST)
        akeys = [("a", pr, n) for n in range(NST)]
        hkey = ("bb", 1 - pr)
        tr.op("ve", lambda v, dd=dd, UO=UO: v.tensor_tensor(
            out=af[:, UO:UO + T],
            in0=af[:, DT + dd * T:DT + (dd + 1) * T],
            in1=af[:, XSC + dd * T:XSC + (dd + 1) * T], op=OP.mult),
            reads=[("dt", dd), ("xsc", dd)], writes=[("u", pr)])
        if dd + 1 < DD:
            emit_agen(c, dd + 1)
        u_b = af[:, UO:UO + T].rearrange(
            "p (o t) -> p o t", o=1).broadcast_to([P, NST, T])
        tr.op("ve", lambda g, u_b=u_b, b_nt=b_nt: g.tensor_tensor(
            out=b_nt, in0=u_b,
            in1=af[:, BSB:BSB + 2048].rearrange("p (n t) -> p n t", n=NST),
            op=OP.mult),
            reads=[("u", pr)] + [("bsb", r) for r in range(4)],
            writes=[("bb", pr)])
        tr.op("ve", lambda v, a_nt=a_nt, FAo=FAo: v.tensor_copy(
            out=af[:, FAo:FAo + 16], in_=a_nt[:, :, 0:1]),
            reads=akeys, writes=[("fixa", pr)])
        tr.op("ve", lambda v, dd=dd, FAo=FAo, FBo=FBo: v.tensor_tensor(
            out=af[:, FBo:FBo + 16], in0=af[:, FAo:FAo + 16],
            in1=af[:, INIT + dd * 16:INIT + (dd + 1) * 16], op=OP.mult),
            reads=[("fixa", pr), ("init", dd)], writes=[("fixb", pr)])
        tr.op("ve", lambda v, b_nt=b_nt, FBo=FBo: v.tensor_tensor(
            out=b_nt[:, :, 0:1], in0=af[:, FBo:FBo + 16],
            in1=b_nt[:, :, 0:1], op=OP.add),
            reads=[("fixb", pr), ("bb", pr)], writes=[("bb", pr)])
        tr.op("ve", lambda v, a_nt=a_nt: v.tensor_scalar(
            out=a_nt[:, :, 0:1], in0=a_nt[:, :, 0:1],
            scalar1=0.0, scalar2=None, op0=OP.mult),
            reads=akeys, writes=akeys)
        tr.op("ve", lambda v, AO=AO, BO=BO, HO=HO: v.tensor_tensor_scan(
            out=af[:, HO:HO + 2048],
            data0=af[:, AO:AO + 2048],
            data1=af[:, BO:BO + 2048],
            initial=0.0, op0=OP.mult, op1=OP.add),
            reads=akeys + [("bb", pr)], writes=[hkey])
        h_nt = af[:, HO:HO + 2048].rearrange("p (n t) -> p n t", n=NST)
        tr.op("ve", lambda v, dd=dd, h_nt=h_nt: v.tensor_copy(
            out=af[:, INIT + dd * 16:INIT + (dd + 1) * 16],
            in_=h_nt[:, :, T - 1:T]),
            reads=[hkey], writes=[("init", dd)])
        tr.op("ve", lambda v, AO=AO, HO=HO: v.tensor_tensor(
            out=af[:, AO:AO + 1536], in0=af[:, HO:HO + 1536],
            in1=ppf[:, 2048:3584], op=OP.mult),
            reads=[hkey] + PS(2048, 3584), writes=akeys[:12])
        tr.op("gp", lambda g, AO=AO, HO=HO: g.tensor_tensor(
            out=af[:, AO + 1536:AO + 2048],
            in0=af[:, HO + 1536:HO + 2048],
            in1=af[:, CEX:CEX + 512], op=OP.mult),
            reads=[hkey, "cex"], writes=akeys[12:])
        tr.op("ve", lambda v, AO=AO, YO=YO: v.tensor_reduce(
            out=af[:, YO:YO + T],
            in_=af[:, AO:AO + 2048].rearrange("p (n t) -> p t n", n=NST),
            axis=AX.X, op=OP.add),
            reads=akeys, writes=[("y", pr)])
        tr.op("ve", lambda v, dd=dd, YO=YO: v.scalar_tensor_tensor(
            out=af[:, YO:YO + T],
            in0=af[:, XSC + dd * T:XSC + (dd + 1) * T],
            scalar=Dv[:, dd:dd + 1], in1=af[:, YO:YO + T],
            op0=OP.mult, op1=OP.add),
            reads=[("xsc", dd), ("Dv", dd), ("y", pr)], writes=[("y", pr)])
        tr.op("ve", lambda v, dd=dd, YO=YO: v.tensor_tensor(
            out=af[:, YO:YO + T], in0=af[:, YO:YO + T],
            in1=ab[:, SVZ + dd * T:SVZ + (dd + 1) * T], op=OP.mult),
            reads=[("y", pr), ("svz", dd)], writes=[("y", pr)])
        tr.op("sc", lambda c_, dd=dd, YO=YO: c_.copy(
            out=ab[:, GTD + dd * T:GTD + (dd + 1) * T],
            in_=af[:, YO:YO + T]),
            reads=[("y", pr)], writes=[("gtd", dd)])

    def tail(c):
        t0 = c * T
        xo = X0 if (c % 2 == 0) else X1
        xkey = ("xslot", c % 2)
        for ns in range(2):
            for dd in range(DD):
                tr.op("te", lambda t, dd=dd, ns=ns: t.matmul(
                    out=ppf[:, ns * 512:(ns + 1) * 512],
                    lhsT=ab[:, GTD + dd * T:GTD + (dd + 1) * T],
                    rhs=wbB[:, dd * 1024 + ns * 512:
                            dd * 1024 + (ns + 1) * 512],
                    start=(dd == 0), stop=(dd == DD - 1)),
                    reads=[("gtd", dd), ("wB", dd)],
                    writes=PS(ns * 512, (ns + 1) * 512))
        tr.op("ve", lambda v, xo=xo: v.tensor_tensor(
            out=af[:, xo:xo + 1024], in0=af[:, xo:xo + 1024],
            in1=ppf[:, 0:1024], op=OP.add),
            reads=[xkey] + PS(0, 1024), writes=[xkey])
        tr.op("ve", lambda v, xo=xo: v.bn_stats(
            out=stats[:, 0:6], in_=af[:, xo:xo + 512]),
            reads=[xkey], writes=["st"])
        tr.op("ve", lambda v, xo=xo: v.bn_stats(
            out=stats[:, 6:12], in_=af[:, xo + 512:xo + 1024]),
            reads=[xkey], writes=["st"])
        tr.op("ve", lambda v: v.bn_aggr(out=stats[:, 12:14],
                                        in_=stats[:, 0:12]),
              reads=["st"], writes=["st"])
        tr.op("sc", lambda c_: c_.activation(
            out=stats[:, 14:15], in_=stats[:, 13:14], func=AF.Sqrt,
            bias=1e-5), reads=["st"], writes=["st"])
        tr.op("ve", lambda v: v.reciprocal(out=stats[:, 15:16],
                                           in_=stats[:, 14:15]),
              reads=["st"], writes=["st"])
        tr.op("ve", lambda v: v.tensor_tensor(
            out=stats[:, 16:17], in0=stats[:, 12:13], in1=stats[:, 15:16],
            op=OP.mult), reads=["st"], writes=["st"])
        tr.op("ve", lambda v: v.tensor_scalar(
            out=stats[:, 16:17], in0=stats[:, 16:17], scalar1=-1.0,
            scalar2=None, op0=OP.mult), reads=["st"], writes=["st"])
        tr.op("sc", lambda c_, xo=xo: c_.activation(
            out=af[:, X1O:X1O + 1024], in_=af[:, xo:xo + 1024],
            func=AF.Identity, scale=stats[:, 15:16], bias=stats[:, 16:17]),
            reads=[xkey, "st"], writes=["x1o"])
        tr.op("gp", lambda g, t0=t0: g.dma_start(
            out=d_x1s[t0:t0 + P, :], in_=af[:, X1O:X1O + 1024]),
            reads=["x1o"], sem="qS")

    pass1_anchor = barrier_anchor()
    for c in range(min(n_mix, NCH)):
        if c == 0:
            loadx(0)
            if n_mix > 1:
                loadx(1)
        front_A(c)
        if c == 0:
            for i in range(DD):
                inproj_pair(0, i)
            for j in range(8):
                conv_pair(0, j)
            front_B(0)
            bcastB(0)
            bcastC(0)
        else:
            for dd in range(DD):
                scan_dd(c - 1, dd)
                inproj_pair(c, dd)
                if dd >= 8:
                    conv_pair(c, dd - 8)
            front_B(c)
            tail(c - 1)
            if c + 1 < min(n_mix, NCH):
                loadx(c + 1)
            bcastB(c)
            bcastC(c)
    if n_mix >= 1:
        for dd in range(DD):
            scan_dd(n_mix - 1, dd)
        tail(n_mix - 1)

    anchor = barrier_anchor()
    for s in range(32):
        tr.op("sync", lambda q, s=s: q.dma_start(
            out=af[:, FSLAB:FSLAB + 1024], in_=d_w1[s * P:(s + 1) * P, :]),
            writes=["pslab"], deps=anchor if s == 0 else (), sem="qL")
        tr.op("sc", lambda c_, s=s: c_.activation(
            out=af[:, FSGN:FSGN + 1024], in_=af[:, FSLAB:FSLAB + 1024],
            func=AF.Abs, accum_out=alpha1[:, s:s + 1]),
            reads=["pslab"], writes=["fsgn", ("al1", s)],
            deps=anchor if s == 0 else ())
        tr.op("sc", lambda c_: c_.activation(
            out=ab[:, 0:1024], in_=af[:, FSLAB:FSLAB + 1024], func=AF.Sign),
            reads=["pslab"], writes=["bslab"],
            deps=anchor if s == 0 else ())
        for kt in range(DMT):
            tr.op("te", lambda t, kt=kt: t.transpose(
                out=ppb[:, kt * P:(kt + 1) * P],
                in_=ab[:, kt * P:(kt + 1) * P], identity=idb[:]),
                reads=["bslab", "idb"], writes=[("pb", kt), "pbB"],
                deps=anchor if s == 0 else ())
        for kt in range(DMT):
            tr.op("sc", lambda c_, s=s, kt=kt: c_.copy(
                out=wbA[:, (kt * 32 + s) * P:(kt * 32 + s + 1) * P],
                in_=ppb[:, kt * P:(kt + 1) * P]),
                reads=[("pb", kt), "pbB"], writes=[("wA", kt * 32 + s)])
        tr.op("sync", lambda q, s=s: q.dma_start(
            out=b1s[:, s:s + 1], in_=d_b1[s * P:(s + 1) * P, :]),
            writes=[("b1", s)], sem="qL")
    al1k = [("al1", s) for s in range(32)]
    tr.op("ve", lambda v: v.tensor_scalar(
        out=alpha1[:], in0=alpha1[:], scalar1=1.0 / 1024.0, scalar2=None,
        op0=OP.mult), reads=al1k, writes=al1k)

    for c2 in range(n_f1):
        t0 = c2 * T2
        for tt in range(2):
            tr.op("sync", lambda q, t0=t0, tt=tt: q.dma_start(
                out=af[:, FX1 + tt * 1024:FX1 + (tt + 1) * 1024],
                in_=d_x1s[t0 + tt * P:t0 + (tt + 1) * P, :]),
                writes=[("fx1", tt)], sem="qL")
            for kt in range(DMT):
                lo = 2048 + kt * P
                tr.op("te", lambda t, kt=kt, tt=tt, lo=lo: t.transpose(
                    out=ppf[:, lo:lo + P],
                    in_=af[:, FX1 + tt * 1024 + kt * P:
                            FX1 + tt * 1024 + (kt + 1) * P],
                    identity=idf[:]),
                    reads=[("fx1", tt), "idf"], writes=PS(lo, lo + P))
            for kt in range(DMT):
                lo = 2048 + kt * P
                tr.op("sc", lambda c_, kt=kt, tt=tt, lo=lo: c_.copy(
                    out=ab[:, FX1T + kt * T2 + tt * P:
                           FX1T + kt * T2 + (tt + 1) * P],
                    in_=ppf[:, lo:lo + P]),
                    reads=PS(lo, lo + P), writes=[("x1T", kt)])
        for mt in range(32):
            po = (mt % 4) * 512
            for kt in range(DMT):
                tr.op("te", lambda t, mt=mt, kt=kt, po=po: t.matmul(
                    out=ppf[:, po:po + 256],
                    lhsT=wbA[:, (kt * 32 + mt) * P:(kt * 32 + mt + 1) * P],
                    rhs=ab[:, FX1T + kt * T2:FX1T + (kt + 1) * T2],
                    start=(kt == 0), stop=(kt == DMT - 1)),
                    reads=[("x1T", kt), ("wA", kt * 32 + mt)],
                    writes=PS(po, po + 256))
            sl = FH1 + (mt % 8) * T2
            tr.op("sc", lambda c_, mt=mt, po=po, sl=sl: c_.activation(
                out=ab[:, sl:sl + T2],
                in_=ppf[:, po:po + 256], func=AF.Gelu,
                scale=alpha1[:, mt:mt + 1], bias=b1s[:, mt:mt + 1]),
                reads=PS(po, po + 256) + [("al1", mt), ("b1", mt)],
                writes=[("h1b", mt % 8)])
            tr.op("gp", lambda g, mt=mt, t0=t0, sl=sl: g.dma_start(
                out=d_h1s[mt * P:(mt + 1) * P, t0:t0 + T2],
                in_=ab[:, sl:sl + T2]),
                reads=[("h1b", mt % 8)], sem="qS")

    anchor2 = barrier_anchor()
    for dmt in range(8):
        for q_ in range(4):
            first = (dmt == 0 and q_ == 0)
            tr.op("sync", lambda q, dmt=dmt, q_=q_: q.dma_start(
                out=af[:, FSLAB:FSLAB + 1024],
                in_=d_w2[dmt * P:(dmt + 1) * P, q_ * 1024:(q_ + 1) * 1024]),
                writes=["pslab"], deps=anchor2 if first else (), sem="qL")
            tr.op("sc", lambda c_, dmt=dmt, q_=q_: c_.activation(
                out=af[:, FSGN:FSGN + 1024], in_=af[:, FSLAB:FSLAB + 1024],
                func=AF.Abs,
                accum_out=alpha2q[:, dmt * 4 + q_:dmt * 4 + q_ + 1]),
                reads=["pslab"], writes=["fsgn", ("a2q", dmt * 4 + q_)],
                deps=anchor2 if first else ())
    a2k = [("a2q", i) for i in range(32)]
    tr.op("ve", lambda v: v.tensor_reduce(
        out=alpha2[:], in_=alpha2q[:].rearrange("p (d q) -> p d q", q=4),
        axis=AX.X, op=OP.add), reads=a2k, writes=["al2"])
    tr.op("ve", lambda v: v.tensor_scalar(
        out=alpha2[:], in0=alpha2[:], scalar1=1.0 / 4096.0, scalar2=None,
        op0=OP.mult), reads=["al2"], writes=["al2"])
    for dmt in range(8):
        for q_ in range(4):
            tr.op("sync", lambda q, dmt=dmt, q_=q_: q.dma_start(
                out=af[:, FSLAB:FSLAB + 1024],
                in_=d_w2[dmt * P:(dmt + 1) * P, q_ * 1024:(q_ + 1) * 1024]),
                writes=["pslab"], sem="qL")
            tr.op("sc", lambda c_: c_.activation(
                out=af[:, FSGN:FSGN + 1024], in_=af[:, FSLAB:FSLAB + 1024],
                func=AF.Sign), reads=["pslab"], writes=["fsgn"])
            tr.op("sc", lambda c_, dmt=dmt: c_.activation(
                out=ab[:, 0:1024], in_=af[:, FSGN:FSGN + 1024],
                func=AF.Copy, scale=alpha2[:, dmt:dmt + 1]),
                reads=["fsgn", "al2"], writes=["bslab"])
            for j in range(DMT):
                tr.op("te", lambda t, j=j: t.transpose(
                    out=ppb[:, j * P:(j + 1) * P],
                    in_=ab[:, j * P:(j + 1) * P], identity=idb[:]),
                    reads=["bslab", "idb"], writes=[("pb", j), "pbB"])
            for j in range(DMT):
                kt = q_ * 8 + j
                tr.op("sc", lambda c_, kt=kt, dmt=dmt, j=j: c_.copy(
                    out=wbA[:, kt * 1024 + dmt * P:kt * 1024 + (dmt + 1) * P],
                    in_=ppb[:, j * P:(j + 1) * P]),
                    reads=[("pb", j), "pbB"], writes=[("wA", kt * 8 + dmt)])
    tr.op("sync", lambda q: q.dma_start(out=wf[0:1, 0:1024], in_=d_b2),
          writes=["b2f"], sem="qL")
    tr.op("sc", lambda c_: c_.copy(out=b2b, in_=wf[0:1, 0:1024]),
          reads=["b2f"], writes=["b2b"])

    for c2 in range(n_f2):
        t0 = c2 * T2
        for tt in range(2):
            tr.op("sync", lambda q, t0=t0, tt=tt: q.dma_start(
                out=af[:, FX1 + tt * 1024:FX1 + (tt + 1) * 1024],
                in_=d_x1s[t0 + tt * P:t0 + (tt + 1) * P, :]),
                writes=[("fx1", tt)], sem="qL")
        for kt in range(32):
            sl = FH1 + (kt % 8) * T2
            tr.op("sync", lambda q, kt=kt, t0=t0, sl=sl: q.dma_start(
                out=ab[:, sl:sl + T2],
                in_=d_h1s[kt * P:(kt + 1) * P, t0:t0 + T2]),
                writes=[("h1b", kt % 8)], sem="qL")
            for tt in range(2):
                for ns in range(2):
                    po = tt * 1024 + ns * 512
                    tr.op("te", lambda t, kt=kt, tt=tt, ns=ns, po=po, sl=sl:
                          t.matmul(
                              out=ppf[:, po:po + 512],
                              lhsT=ab[:, sl + tt * P:sl + (tt + 1) * P],
                              rhs=wbA[:, kt * 1024 + ns * 512:
                                      kt * 1024 + (ns + 1) * 512],
                              start=(kt == 0), stop=False),
                          reads=[("h1b", kt % 8)] +
                                [("wA", kt * 8 + ns * 4 + i)
                                 for i in range(4)],
                          writes=PS(po, po + 512))
        for tt in range(2):
            for ns in range(2):
                po = tt * 1024 + ns * 512
                tr.op("te", lambda t, ns=ns, po=po: t.matmul(
                    out=ppf[:, po:po + 512], lhsT=onesb[:],
                    rhs=b2b[:, ns * 512:(ns + 1) * 512],
                    start=False, stop=True),
                    reads=["b2b", "onesb"], writes=PS(po, po + 512))
        for tt in range(2):
            tr.op("ve", lambda v, tt=tt: v.tensor_tensor(
                out=af[:, FX1 + tt * 1024:FX1 + (tt + 1) * 1024],
                in0=af[:, FX1 + tt * 1024:FX1 + (tt + 1) * 1024],
                in1=ppf[:, tt * 1024:(tt + 1) * 1024], op=OP.add),
                reads=[("fx1", tt)] + PS(tt * 1024, (tt + 1) * 1024),
                writes=[("fx1", tt)])
            tr.op("ve", lambda v, tt=tt: v.bn_stats(
                out=stats[:, 0:6],
                in_=af[:, FX1 + tt * 1024:FX1 + tt * 1024 + 512]),
                reads=[("fx1", tt)], writes=["st"])
            tr.op("ve", lambda v, tt=tt: v.bn_stats(
                out=stats[:, 6:12],
                in_=af[:, FX1 + tt * 1024 + 512:FX1 + (tt + 1) * 1024]),
                reads=[("fx1", tt)], writes=["st"])
            tr.op("ve", lambda v: v.bn_aggr(out=stats[:, 12:14],
                                            in_=stats[:, 0:12]),
                  reads=["st"], writes=["st"])
            tr.op("sc", lambda c_: c_.activation(
                out=stats[:, 14:15], in_=stats[:, 13:14], func=AF.Sqrt,
                bias=1e-5), reads=["st"], writes=["st"])
            tr.op("ve", lambda v: v.reciprocal(out=stats[:, 15:16],
                                               in_=stats[:, 14:15]),
                  reads=["st"], writes=["st"])
            tr.op("ve", lambda v: v.tensor_tensor(
                out=stats[:, 16:17], in0=stats[:, 12:13],
                in1=stats[:, 15:16], op=OP.mult),
                reads=["st"], writes=["st"])
            tr.op("ve", lambda v: v.tensor_scalar(
                out=stats[:, 16:17], in0=stats[:, 16:17], scalar1=-1.0,
                scalar2=None, op0=OP.mult), reads=["st"], writes=["st"])
            tr.op("sc", lambda c_, tt=tt: c_.activation(
                out=af[:, FOUT + tt * 1024:FOUT + (tt + 1) * 1024],
                in_=af[:, FX1 + tt * 1024:FX1 + (tt + 1) * 1024],
                func=AF.Identity, scale=stats[:, 15:16],
                bias=stats[:, 16:17]),
                reads=[("fx1", tt), "st"], writes=[("fo", tt)])
            tr.op("gp", lambda g, t0=t0, tt=tt: g.dma_start(
                out=d_out[t0 + tt * P:t0 + (tt + 1) * P, :],
                in_=af[:, FOUT + tt * 1024:FOUT + (tt + 1) * 1024]),
                reads=[("fo", tt)], sem="qS")

    if max_ops is not None and max_ops < len(prog.ops):
        prog.ops = prog.ops[:max_ops]
        prog.cnt = {}
        for i in range(max_ops):
            s = prog.seq[i][0]
            prog.cnt[s] = max(prog.cnt.get(s, 0), prog.seq[i][1])
    final_counts = [(s, prog.cnt.get(s, 0)) for s in sem_names
                    if s.startswith("q")]

    with nc.Block() as blk:
        @blk.tensor
        def _(te):
            prog.emit_for("te", te, sems, mult)

        @blk.scalar
        def _(sc):
            prog.emit_for("sc", sc, sems, mult)

        @blk.vector
        def _(ve):
            prog.emit_for("ve", ve, sems, mult)

        @blk.gpsimd
        def _(gp):
            prog.emit_for("gp", gp, sems, mult)

        @blk.sync
        def _(sy):
            prog.emit_for("sync", sy, sems, mult)
            for s, cnt in final_counts:
                if cnt:
                    sy.wait_ge(sems[s], 16 * cnt)

    return nc


_NC_CACHE = None


def kernel(L0_attn_x, L0_attn_in_proj_w, L0_attn_conv_w, L0_attn_conv_b,
           L0_attn_x_proj_w, L0_attn_dt_proj_w, L0_attn_dt_proj_b,
           L0_attn_A_log, L0_attn_D, L0_attn_out_proj_w,
           L0_ffn_w1, L0_ffn_b1, L0_ffn_w2, L0_ffn_b2):
    global _NC_CACHE
    if _NC_CACHE is None:
        _NC_CACHE = build()
    nc = _NC_CACHE

    def f32c(a):
        return np.ascontiguousarray(np.asarray(a), dtype=np.float32)

    shared = dict(
        inw=f32c(L0_attn_in_proj_w),
        cw=f32c(L0_attn_conv_w),
        cb=f32c(L0_attn_conv_b).reshape(DIN, 1),
        xpw=f32c(L0_attn_x_proj_w),
        dtw=f32c(L0_attn_dt_proj_w),
        dtb=f32c(L0_attn_dt_proj_b).reshape(DIN, 1),
        alog=f32c(L0_attn_A_log),
        dv=f32c(L0_attn_D).reshape(DIN, 1),
        outw=f32c(L0_attn_out_proj_w),
        w1=f32c(L0_ffn_w1),
        b1=f32c(L0_ffn_b1).reshape(4096, 1),
        w2=f32c(L0_ffn_w2),
        b2=f32c(L0_ffn_b2).reshape(1, DM),
    )
    xf = f32c(L0_attn_x)
    in_maps = [dict(shared, x=np.ascontiguousarray(xf[b])) for b in range(8)]
    res = run_bass_kernel_spmd(nc, in_maps, core_ids=list(range(8)))
    return np.stack([np.asarray(res.results[b]["out"]) for b in range(8)],
                    axis=0)


if __name__ == "__main__":
    import reference as R
    inp = R.setup_inputs(0)
    inp = {k: np.asarray(v) for k, v in inp.items()}
    out = kernel(**inp)
    ref = np.asarray(R.reference(**inp))
    err = np.mean((out - ref) ** 2) / np.mean(ref ** 2)
    print("resvar", err, "maxabs", np.abs(out - ref).max())


# version 11
# speedup vs baseline: 1.0098x; 1.0098x over previous
import numpy as np
import concourse.bass as bass
import concourse.mybir as mybir
from concourse.bass_utils import run_bass_kernel_spmd

F32 = mybir.dt.float32
BF16 = mybir.dt.bfloat16
AF = mybir.ActivationFunctionType
OP = mybir.AluOpType
AX = mybir.AxisListType

P = 128
T = 128
NCH = 32
T2 = 256
NCH2 = 16
DM = 1024
DMT = 8
DIN = 2048
DD = 16
NST = 16
RNK = 64
L = 4096


def PS(lo, hi):
    return [("ps", u) for u in range(lo // 512, (hi + 511) // 512)]


class Prog:
    def __init__(self):
        self.ops = []
        self.seq = {}
        self.cnt = {}

    def add(self, eng, emit, deps=(), sem=None):
        sem = sem or eng
        self.cnt[sem] = self.cnt.get(sem, 0) + 1
        i = len(self.ops)
        self.ops.append((eng, emit, tuple(d for d in deps if d is not None)))
        self.seq[i] = (sem, self.cnt[sem])
        return i

    def emit_for(self, ename, eng, sems, mult):
        waited = {}
        for i, (e, emitfn, deps) in enumerate(self.ops):
            if e != ename:
                continue
            mysem = self.seq[i][0]
            for d in deps:
                dsem, dcnt = self.seq[d]
                tgt = dcnt * mult[dsem]
                if waited.get(dsem, -1) < tgt:
                    eng.wait_ge(sems[dsem], tgt)
                    waited[dsem] = tgt
            last = emitfn(eng)
            last.then_inc(sems[mysem], mult[mysem])


NLANES_L = 8
NLANES_S = 4


class Tr:
    def __init__(self, prog):
        self.p = prog
        self.w = {}
        self.r = {}
        self.lane_idx = {"qL": 0, "qS": 0}
        self.lane_last = {}

    def op(self, eng, emit, reads=(), writes=(), deps=(), sem=None):
        dl = list(deps)
        for k in reads:
            if k in self.w:
                dl.append(self.w[k])
        for k in writes:
            dl += self.r.get(k, [])
            if k in self.w:
                dl.append(self.w[k])
        if sem in ("qL", "qS"):
            nl = NLANES_L if sem == "qL" else NLANES_S
            li = self.lane_idx[sem] % nl
            self.lane_idx[sem] += 1
            lane = f"{sem}{li}"
            if lane in self.lane_last:
                dl.append(self.lane_last[lane])
            i = self.p.add(eng, emit, dl, sem=lane)
            self.lane_last[lane] = i
        else:
            i = self.p.add(eng, emit, dl, sem=sem)
        for k in reads:
            self.r.setdefault(k, []).append(i)
        for k in writes:
            self.w[k] = i
            self.r[k] = []
        return i


def build(n_mix=NCH, n_f1=NCH2, n_f2=NCH2, max_ops=None):
    nc = bass.Bass("TRN2", target_bir_lowering=False)

    d_x = nc.dram_tensor("x", [L, DM], F32, kind="ExternalInput").ap()
    d_inw = nc.dram_tensor("inw", [2 * DIN, DM], F32, kind="ExternalInput").ap()
    d_cw = nc.dram_tensor("cw", [DIN, 4], F32, kind="ExternalInput").ap()
    d_cb = nc.dram_tensor("cb", [DIN, 1], F32, kind="ExternalInput").ap()
    d_xpw = nc.dram_tensor("xpw", [96, DIN], F32, kind="ExternalInput").ap()
    d_dtw = nc.dram_tensor("dtw", [DIN, RNK], F32, kind="ExternalInput").ap()
    d_dtb = nc.dram_tensor("dtb", [DIN, 1], F32, kind="ExternalInput").ap()
    d_alog = nc.dram_tensor("alog", [DIN, NST], F32, kind="ExternalInput").ap()
    d_dv = nc.dram_tensor("dv", [DIN, 1], F32, kind="ExternalInput").ap()
    d_outw = nc.dram_tensor("outw", [DM, DIN], F32, kind="ExternalInput").ap()
    d_w1 = nc.dram_tensor("w1", [4096, DM], F32, kind="ExternalInput").ap()
    d_b1 = nc.dram_tensor("b1", [4096, 1], F32, kind="ExternalInput").ap()
    d_w2 = nc.dram_tensor("w2", [DM, 4096], F32, kind="ExternalInput").ap()
    d_b2 = nc.dram_tensor("b2", [1, DM], F32, kind="ExternalInput").ap()
    d_out = nc.dram_tensor("out", [L, DM], F32, kind="ExternalOutput").ap()
    d_x1s = nc.dram_tensor("x1s", [L, DM], F32, kind="Internal").ap()
    d_h1s = nc.dram_tensor("h1s", [4096, L], BF16, kind="Internal").ap()

    wbA = nc.alloc_sbuf_tensor("wbA", [P, 32768], BF16)
    wbB = nc.alloc_sbuf_tensor("wbB", [P, 16384], BF16)
    wf = nc.alloc_sbuf_tensor("wf", [P, 3584], F32)
    af = nc.alloc_sbuf_tensor("af", [P, 21504], F32)
    ab = nc.alloc_sbuf_tensor("ab", [P, 5120], BF16)
    sm = nc.alloc_sbuf_tensor("sm", [P, 512], F32)
    onesf = nc.alloc_sbuf_tensor("onesf", [1, P], F32)
    onesb = nc.alloc_sbuf_tensor("onesb", [1, P], BF16)
    idf = nc.alloc_sbuf_tensor("idf", [P, P], F32)
    idb = nc.alloc_sbuf_tensor("idb", [P, P], BF16)

    b2b = ab[0:1, 4096:4096 + DM]
    ppf = nc.alloc_psum_tensor("ppf", [P, 3584], F32)
    ppb = nc.alloc_psum_tensor("ppb", [P, 1024], BF16)

    convw = sm[:, 0:64]
    convb = sm[:, 64:80]
    dtbv = sm[:, 80:96]
    Dv = sm[:, 96:112]
    alpha1 = sm[:, 112:144]
    b1s = sm[:, 144:176]
    alpha2q = sm[:, 176:208]
    alpha2 = sm[:, 208:216]
    stats = sm[:, 216:240]
    Aneg = sm[:, 240:496]
    eps1 = sm[:, 496:497]
    nc.const_aps.register(F32, 1e-5, eps1[:])

    X0, X1 = 0, 1024
    XSP = 2048
    XSC = 4144
    DT = 6192
    ABUF = 8240
    BBUF = 10288
    HBUF = 12336
    YDD = 14384
    UDD = 14512
    FIXA = 14640
    FIXB = 14656
    INIT = 14672
    DTR = 14928
    BCS = 15056
    CEX = 15184
    AB_ = [8240, 15696]
    BB_ = [10288, 17744]
    YD_ = [14384, 19792]
    UD_ = [14512, 19920]
    FA_ = [14640, 20048]
    FB_ = [14656, 20064]
    X1O = 20480
    FX1 = 0
    FOUT = 2048
    FSLAB = 4096
    FSGN = 5120
    XT = 0
    SVZ = 1024
    GTD = 3072
    FX1T = 0
    FH1 = 2048

    sems = {}
    sem_names = (["te", "sc", "ve", "gp"] +
                 [f"qL{i}" for i in range(NLANES_L)] +
                 [f"qS{i}" for i in range(NLANES_S)])
    for s in sem_names:
        sems[s] = nc.alloc_semaphore(s)
    mult = {s: (16 if s.startswith("q") else 1) for s in sem_names}

    prog = Prog()
    tr = Tr(prog)

    def barrier_anchor():
        last = {}
        for i in range(len(prog.ops)):
            last[prog.seq[i][0]] = i
        return list(last.values())

    tr.op("gp", lambda g: g.memset(idf[:], 0.0), writes=["idf"])
    tr.op("gp", lambda g: g.affine_select(
        out=idf[:], in_=idf[:], compare_op=OP.not_equal, fill=1.0,
        base=0, pattern=[[-1, P]], channel_multiplier=1),
        reads=["idf"], writes=["idf"])
    tr.op("gp", lambda g: g.memset(idb[:], 0.0), writes=["idb"])
    tr.op("gp", lambda g: g.affine_select(
        out=idb[:], in_=idb[:], compare_op=OP.not_equal, fill=1.0,
        base=0, pattern=[[-1, P]], channel_multiplier=1),
        reads=["idb"], writes=["idb"])
    tr.op("gp", lambda g: g.memset(onesf[:], 1.0), writes=["onesf"])
    tr.op("gp", lambda g: g.memset(onesb[:], 1.0), writes=["onesb"])
    tr.op("gp", lambda g: g.memset(eps1[:], 1e-5), writes=["st"])
    tr.op("ve", lambda v: v.memset(af[:, XSP:XSP + 2096], 0.0),
          writes=[("xsp", dd) for dd in range(DD)])
    tr.op("ve", lambda v: v.memset(af[:, INIT:INIT + 256], 0.0),
          writes=[("init", dd) for dd in range(DD)])

    for dd in range(DD):
        tr.op("sync", lambda s, dd=dd: s.dma_start(
            out=convw[:, dd * 4:dd * 4 + 4], in_=d_cw[dd * P:(dd + 1) * P, :]),
            writes=[("convw", dd)], sem="qL")
        tr.op("sync", lambda s, dd=dd: s.dma_start(
            out=convb[:, dd:dd + 1], in_=d_cb[dd * P:(dd + 1) * P, :]),
            writes=[("convb", dd)], sem="qL")
        tr.op("sync", lambda s, dd=dd: s.dma_start(
            out=dtbv[:, dd:dd + 1], in_=d_dtb[dd * P:(dd + 1) * P, :]),
            writes=[("dtbv", dd)], sem="qL")
        tr.op("sync", lambda s, dd=dd: s.dma_start(
            out=Dv[:, dd:dd + 1], in_=d_dv[dd * P:(dd + 1) * P, :]),
            writes=[("Dv", dd)], sem="qL")
        tr.op("sync", lambda s, dd=dd: s.dma_start(
            out=Aneg[:, dd * 16:dd * 16 + 16],
            in_=d_alog[dd * P:(dd + 1) * P, :]),
            writes=[("An", dd)], sem="qL")
    allA = [("An", dd) for dd in range(DD)]
    tr.op("sc", lambda c: c.activation(out=Aneg[:], in_=Aneg[:], func=AF.Exp),
          reads=allA, writes=allA)
    tr.op("ve", lambda v: v.tensor_scalar(
        out=Aneg[:], in0=Aneg[:], scalar1=-1.0, scalar2=None, op0=OP.mult),
        reads=allA, writes=allA)

    for s in range(32):
        tr.op("sync", lambda q, s=s: q.dma_start(
            out=af[:, 0:1024], in_=d_inw[s * P:(s + 1) * P, :]),
            writes=["pslab"], sem="qL")
        tr.op("sc", lambda c: c.copy(out=ab[:, 0:1024], in_=af[:, 0:1024]),
              reads=["pslab"], writes=["bslab"])
        for kt in range(DMT):
            tr.op("te", lambda t, kt=kt: t.transpose(
                out=ppb[:, kt * P:(kt + 1) * P],
                in_=ab[:, kt * P:(kt + 1) * P], identity=idb[:]),
                reads=["bslab", "idb"], writes=[("pb", kt), "pbB"])
        for kt in range(DMT):
            tr.op("sc", lambda c, s=s, kt=kt: c.copy(
                out=wbA[:, (kt * 32 + s) * P:(kt * 32 + s + 1) * P],
                in_=ppb[:, kt * P:(kt + 1) * P]),
                reads=[("pb", kt), "pbB"], writes=[("wA", kt * 32 + s)])

    for s in range(8):
        tr.op("sync", lambda q, s=s: q.dma_start(
            out=af[:, 0:2048], in_=d_outw[s * P:(s + 1) * P, :]),
            writes=["pslab"], sem="qL")
        tr.op("sc", lambda c: c.copy(out=ab[:, 0:2048], in_=af[:, 0:2048]),
              reads=["pslab"], writes=["bslab"])
        for half in range(2):
            for j in range(8):
                kt = half * 8 + j
                tr.op("te", lambda t, kt=kt, j=j: t.transpose(
                    out=ppb[:, j * P:(j + 1) * P],
                    in_=ab[:, kt * P:(kt + 1) * P], identity=idb[:]),
                    reads=["bslab", "idb"], writes=[("pb", j), "pbB"])
            for j in range(8):
                kt = half * 8 + j
                tr.op("sc", lambda c, s=s, kt=kt, j=j: c.copy(
                    out=wbB[:, kt * 1024 + s * P:kt * 1024 + (s + 1) * P],
                    in_=ppb[:, j * P:(j + 1) * P]),
                    reads=[("pb", j), "pbB"], writes=[("wB", kt)])

    tr.op("sync", lambda q: q.dma_start(out=af[0:96, 0:2048], in_=d_xpw),
          writes=["pslab"], sem="qL")
    for kt in range(DD):
        lo = (kt % 2) * 512
        tr.op("te", lambda t, kt=kt, lo=lo: t.transpose(
            out=ppf[:, lo:lo + 96],
            in_=af[0:96, kt * P:(kt + 1) * P], identity=idf[0:96, 0:96]),
            reads=["pslab", "idf"], writes=PS(lo, lo + 512))
        tr.op("sc", lambda c, kt=kt, lo=lo: c.copy(
            out=wf[:, kt * 96:(kt + 1) * 96], in_=ppf[:, lo:lo + 96]),
            reads=PS(lo, lo + 512), writes=[("wxp", kt)])

    for dd in range(DD):
        lo = (dd % 2) * 512
        tr.op("sync", lambda q, dd=dd: q.dma_start(
            out=af[:, 0:64], in_=d_dtw[dd * P:(dd + 1) * P, :]),
            writes=["pslab"], sem="qL")
        tr.op("te", lambda t, dd=dd, lo=lo: t.transpose(
            out=ppf[0:64, lo:lo + P], in_=af[:, 0:64], identity=idf[:]),
            reads=["pslab", "idf"], writes=PS(lo, lo + 512))
        tr.op("sc", lambda c, dd=dd, lo=lo: c.copy(
            out=wf[0:64, 1536 + dd * P:1536 + (dd + 1) * P],
            in_=ppf[0:64, lo:lo + P]),
            reads=PS(lo, lo + 512), writes=[("wdt", dd)])

    def xs_ap(dd, lo, hi):
        return af[:, XSP + dd * 131 + lo:XSP + dd * 131 + hi]

    BSB = HBUF

    def front_A(c):
        t0 = c * T
        xo = X0 if (c % 2 == 0) else X1
        xkey = ("xslot", c % 2)
        tr.op("sync", lambda q, t0=t0, xo=xo: q.dma_start(
            out=af[:, xo:xo + 1024], in_=d_x[t0:t0 + P, :]),
            writes=[xkey], deps=pass1_anchor if c == 0 else (), sem="qL")
        for kt in range(DMT):
            tr.op("te", lambda t, kt=kt, xo=xo: t.transpose(
                out=ppf[:, kt * P:(kt + 1) * P],
                in_=af[:, xo + kt * P:xo + (kt + 1) * P], identity=idf[:]),
                reads=[xkey, "idf"], writes=PS(kt * P, (kt + 1) * P))
        for kt in range(DMT):
            tr.op("sc", lambda c_, kt=kt: c_.copy(
                out=ab[:, XT + kt * P:XT + (kt + 1) * P],
                in_=ppf[:, kt * P:(kt + 1) * P]),
                reads=PS(kt * P, (kt + 1) * P), writes=[("xT", kt)])

    def inproj_pair(c, i):
        for mt in (2 * i, 2 * i + 1):
            po = 1024 + (mt % 2) * 512
            for kt in range(DMT):
                tr.op("te", lambda t, mt=mt, kt=kt, po=po: t.matmul(
                    out=ppf[:, po:po + P],
                    lhsT=wbA[:, (kt * 32 + mt) * P:(kt * 32 + mt + 1) * P],
                    rhs=ab[:, XT + kt * P:XT + (kt + 1) * P],
                    start=(kt == 0), stop=(kt == DMT - 1)),
                    reads=[("xT", kt), ("wA", kt * 32 + mt)],
                    writes=PS(po, po + P))
            if mt < DD:
                tr.op("sc", lambda c_, mt=mt, po=po: c_.copy(
                    out=xs_ap(mt, 3, 131), in_=ppf[:, po:po + P]),
                    reads=PS(po, po + P), writes=[("xsp", mt)])
            else:
                tr.op("sc", lambda c_, mt=mt, po=po: c_.activation(
                    out=ab[:, SVZ + (mt - DD) * T:SVZ + (mt - DD + 1) * T],
                    in_=ppf[:, po:po + P], func=AF.Silu),
                    reads=PS(po, po + P), writes=[("svz", mt - DD)])

    def conv_pair(c, j):
        for dd in (2 * j, 2 * j + 1):
            pr = dd % 2
            UO, YO = UD_[pr], YD_[pr]
            tr.op("ve", lambda v, dd=dd, UO=UO: v.tensor_scalar(
                out=af[:, UO:UO + T], in0=xs_ap(dd, 0, T),
                scalar1=convw[:, dd * 4:dd * 4 + 1], scalar2=None,
                op0=OP.mult),
                reads=[("xsp", dd), ("convw", dd)], writes=[("u", pr)])
            tr.op("ve", lambda v, dd=dd, UO=UO, YO=YO: v.scalar_tensor_tensor(
                out=af[:, YO:YO + T], in0=xs_ap(dd, 1, 1 + T),
                scalar=convw[:, dd * 4 + 1:dd * 4 + 2],
                in1=af[:, UO:UO + T], op0=OP.mult, op1=OP.add),
                reads=[("xsp", dd), ("u", pr), ("convw", dd)],
                writes=[("y", pr)])
            tr.op("ve", lambda v, dd=dd, UO=UO, YO=YO: v.scalar_tensor_tensor(
                out=af[:, UO:UO + T], in0=xs_ap(dd, 2, 2 + T),
                scalar=convw[:, dd * 4 + 2:dd * 4 + 3],
                in1=af[:, YO:YO + T], op0=OP.mult, op1=OP.add),
                reads=[("xsp", dd), ("y", pr), ("convw", dd)],
                writes=[("u", pr)])
            tr.op("ve", lambda v, dd=dd, UO=UO, YO=YO: v.scalar_tensor_tensor(
                out=af[:, YO:YO + T], in0=xs_ap(dd, 3, 3 + T),
                scalar=convw[:, dd * 4 + 3:dd * 4 + 4],
                in1=af[:, UO:UO + T], op0=OP.mult, op1=OP.add),
                reads=[("xsp", dd), ("u", pr), ("convw", dd)],
                writes=[("y", pr)])
            tr.op("sc", lambda c_, dd=dd, YO=YO: c_.activation(
                out=af[:, XSC + dd * T:XSC + (dd + 1) * T],
                in_=af[:, YO:YO + T], func=AF.Silu,
                bias=convb[:, dd:dd + 1]),
                reads=[("y", pr), ("convb", dd)], writes=[("xsc", dd)])
            tr.op("ve", lambda v, dd=dd: v.tensor_copy(
                out=xs_ap(dd, 0, 3), in_=xs_ap(dd, 128, 131)),
                reads=[("xsp", dd)], writes=[("xsp", dd)])

    def emit_agen(c, dd):
        pr = dd % 2
        AO = AB_[pr]
        for n in range(NST):
            tr.op("sc", lambda c_, dd=dd, n=n, AO=AO: c_.activation(
                out=af[:, AO + n * T:AO + (n + 1) * T],
                in_=af[:, DT + dd * T:DT + (dd + 1) * T], func=AF.Exp,
                scale=Aneg[:, dd * 16 + n:dd * 16 + n + 1]),
                reads=[("dt", dd), ("An", dd)], writes=[("a", pr, n)])

    def front_B(c):
        for kt in range(DD):
            tr.op("te", lambda t, kt=kt: t.matmul(
                out=ppf[0:96, 0:P],
                lhsT=wf[:, kt * 96:(kt + 1) * 96],
                rhs=af[:, XSC + kt * T:XSC + (kt + 1) * T],
                start=(kt == 0), stop=(kt == DD - 1)),
                reads=[("xsc", kt), ("wxp", kt)], writes=PS(0, 512))
        tr.op("sc", lambda c_: c_.copy(out=af[0:64, DTR:DTR + P],
                                       in_=ppf[0:64, 0:P]),
              reads=PS(0, 512), writes=["dtr"])
        tr.op("sc", lambda c_: c_.copy(out=af[0:32, BCS:BCS + P],
                                       in_=ppf[64:96, 0:P]),
              reads=PS(0, 512), writes=["bcs"])
        for dd in range(DD):
            po = (dd % 2) * 512
            spo = YD_[dd % 2]
            tr.op("te", lambda t, dd=dd, po=po: t.matmul(
                out=ppf[:, po:po + P],
                lhsT=wf[0:64, 1536 + dd * P:1536 + (dd + 1) * P],
                rhs=af[0:64, DTR:DTR + P], start=True, stop=True),
                reads=["dtr", ("wdt", dd)], writes=PS(po, po + 512))
            tr.op("ve", lambda v, dd=dd, po=po: v.tensor_scalar(
                out=af[:, DT + dd * T:DT + (dd + 1) * T],
                in0=ppf[:, po:po + P], scalar1=dtbv[:, dd:dd + 1],
                scalar2=80.0, op0=OP.add, op1=OP.min),
                reads=PS(po, po + 512) + [("dtbv", dd)], writes=[("dt", dd)])
            tr.op("sc", lambda c_, dd=dd, spo=spo: c_.activation(
                out=af[:, spo:spo + T],
                in_=af[:, DT + dd * T:DT + (dd + 1) * T], func=AF.Exp),
                reads=[("dt", dd)], writes=[("y", dd % 2)])
            tr.op("sc", lambda c_, dd=dd, spo=spo: c_.activation(
                out=af[:, DT + dd * T:DT + (dd + 1) * T],
                in_=af[:, spo:spo + T], func=AF.Ln, bias=1.0),
                reads=[("y", dd % 2)], writes=[("dt", dd)])
            if dd == 0:
                emit_agen(c, 0)

    def sel(row):
        return idf[0:32, row:row + 1].broadcast_to([32, P])

    def bcastB(c):
        for r in range(4):
            lo = (r % 2) * 512
            for j in range(4):
                n = r * 4 + j
                tr.op("te", lambda t, n=n, lo=lo, j=j: t.matmul(
                    out=ppf[:, lo + j * P:lo + (j + 1) * P], lhsT=sel(n),
                    rhs=af[0:32, BCS:BCS + P], start=True, stop=True),
                    reads=["bcs", "idf"], writes=PS(lo, lo + 512))
            tr.op("sc", lambda c_, r=r, lo=lo: c_.copy(
                out=af[:, BSB + r * 512:BSB + (r + 1) * 512],
                in_=ppf[:, lo:lo + 512]),
                reads=PS(lo, lo + 512), writes=[("bsb", r)])

    def bcastC(c):
        for j in range(4):
            lo = 2048 + j * P
            tr.op("te", lambda t, j=j, lo=lo: t.matmul(
                out=ppf[:, lo:lo + P], lhsT=sel(28 + j),
                rhs=af[0:32, BCS:BCS + P], start=True, stop=True),
                reads=["bcs", "idf"], writes=PS(2048, 2560))
        tr.op("sc", lambda c_: c_.copy(out=af[:, CEX:CEX + 512],
                                       in_=ppf[:, 2048:2560]),
              reads=PS(2048, 2560), writes=["cex"])
        for n in range(12):
            lo = 2048 + n * P
            tr.op("te", lambda t, n=n, lo=lo: t.matmul(
                out=ppf[:, lo:lo + P], lhsT=sel(16 + n),
                rhs=af[0:32, BCS:BCS + P], start=True, stop=True),
                reads=["bcs", "idf"], writes=PS(lo, lo + P))

    def scan_dd(c, dd):
        pr = dd % 2
        AO, BO, YO, UO = AB_[pr], BB_[pr], YD_[pr], UD_[pr]
        HO = BB_[1 - pr]
        FAo, FBo = FA_[pr], FB_[pr]
        a_nt = af[:, AO:AO + 2048].rearrange("p (n t) -> p n t", n=NST)
        b_nt = af[:, BO:BO + 2048].rearrange("p (n t) -> p n t", n=NST)
        akeys = [("a", pr, n) for n in range(NST)]
        hkey = ("bb", 1 - pr)
        tr.op("ve", lambda v, dd=dd, UO=UO: v.tensor_tensor(
            out=af[:, UO:UO + T],
            in0=af[:, DT + dd * T:DT + (dd + 1) * T],
            in1=af[:, XSC + dd * T:XSC + (dd + 1) * T], op=OP.mult),
            reads=[("dt", dd), ("xsc", dd)], writes=[("u", pr)])
        if dd + 1 < DD:
            emit_agen(c, dd + 1)
        u_b = af[:, UO:UO + T].rearrange(
            "p (o t) -> p o t", o=1).broadcast_to([P, NST, T])
        tr.op("ve", lambda g, u_b=u_b, b_nt=b_nt: g.tensor_tensor(
            out=b_nt, in0=u_b,
            in1=af[:, BSB:BSB + 2048].rearrange("p (n t) -> p n t", n=NST),
            op=OP.mult),
            reads=[("u", pr)] + [("bsb", r) for r in range(4)],
            writes=[("bb", pr)])
        tr.op("ve", lambda v, dd=dd, a_nt=a_nt, FBo=FBo: v.tensor_tensor(
            out=af[:, FBo:FBo + 16], in0=a_nt[:, :, 0:1],
            in1=af[:, INIT + dd * 16:INIT + (dd + 1) * 16], op=OP.mult),
            reads=akeys + [("init", dd)], writes=[("fixb", pr)])
        tr.op("ve", lambda v, b_nt=b_nt, FBo=FBo: v.tensor_tensor(
            out=b_nt[:, :, 0:1], in0=af[:, FBo:FBo + 16],
            in1=b_nt[:, :, 0:1], op=OP.add),
            reads=[("fixb", pr), ("bb", pr)], writes=[("bb", pr)])
        tr.op("ve", lambda v, a_nt=a_nt: v.tensor_scalar(
            out=a_nt[:, :, 0:1], in0=a_nt[:, :, 0:1],
            scalar1=0.0, scalar2=None, op0=OP.mult),
            reads=akeys, writes=akeys)
        tr.op("ve", lambda v, AO=AO, BO=BO, HO=HO: v.tensor_tensor_scan(
            out=af[:, HO:HO + 2048],
            data0=af[:, AO:AO + 2048],
            data1=af[:, BO:BO + 2048],
            initial=0.0, op0=OP.mult, op1=OP.add),
            reads=akeys + [("bb", pr)], writes=[hkey])
        h_nt = af[:, HO:HO + 2048].rearrange("p (n t) -> p n t", n=NST)
        tr.op("ve", lambda v, dd=dd, h_nt=h_nt: v.tensor_copy(
            out=af[:, INIT + dd * 16:INIT + (dd + 1) * 16],
            in_=h_nt[:, :, T - 1:T]),
            reads=[hkey], writes=[("init", dd)])
        tr.op("ve", lambda v, AO=AO, HO=HO: v.tensor_tensor(
            out=af[:, AO:AO + 1536], in0=af[:, HO:HO + 1536],
            in1=ppf[:, 2048:3584], op=OP.mult),
            reads=[hkey] + PS(2048, 3584), writes=akeys[:12])
        tr.op("gp", lambda g, AO=AO, HO=HO: g.tensor_tensor(
            out=af[:, AO + 1536:AO + 2048],
            in0=af[:, HO + 1536:HO + 2048],
            in1=af[:, CEX:CEX + 512], op=OP.mult),
            reads=[hkey, "cex"], writes=akeys[12:])
        tr.op("ve", lambda v, AO=AO, YO=YO: v.tensor_reduce(
            out=af[:, YO:YO + T],
            in_=af[:, AO:AO + 2048].rearrange("p (n t) -> p t n", n=NST),
            axis=AX.X, op=OP.add),
            reads=akeys, writes=[("y", pr)])
        tr.op("ve", lambda v, dd=dd, YO=YO: v.scalar_tensor_tensor(
            out=af[:, YO:YO + T],
            in0=af[:, XSC + dd * T:XSC + (dd + 1) * T],
            scalar=Dv[:, dd:dd + 1], in1=af[:, YO:YO + T],
            op0=OP.mult, op1=OP.add),
            reads=[("xsc", dd), ("Dv", dd), ("y", pr)], writes=[("y", pr)])
        tr.op("ve", lambda v, dd=dd, YO=YO: v.tensor_tensor(
            out=af[:, YO:YO + T], in0=af[:, YO:YO + T],
            in1=ab[:, SVZ + dd * T:SVZ + (dd + 1) * T], op=OP.mult),
            reads=[("y", pr), ("svz", dd)], writes=[("y", pr)])
        tr.op("sc", lambda c_, dd=dd, YO=YO: c_.copy(
            out=ab[:, GTD + dd * T:GTD + (dd + 1) * T],
            in_=af[:, YO:YO + T]),
            reads=[("y", pr)], writes=[("gtd", dd)])

    def tail(c):
        t0 = c * T
        xo = X0 if (c % 2 == 0) else X1
        xkey = ("xslot", c % 2)
        for ns in range(2):
            for dd in range(DD):
                tr.op("te", lambda t, dd=dd, ns=ns: t.matmul(
                    out=ppf[:, ns * 512:(ns + 1) * 512],
                    lhsT=ab[:, GTD + dd * T:GTD + (dd + 1) * T],
                    rhs=wbB[:, dd * 1024 + ns * 512:
                            dd * 1024 + (ns + 1) * 512],
                    start=(dd == 0), stop=(dd == DD - 1)),
                    reads=[("gtd", dd), ("wB", dd)],
                    writes=PS(ns * 512, (ns + 1) * 512))
        tr.op("ve", lambda v, xo=xo: v.tensor_tensor(
            out=af[:, xo:xo + 1024], in0=af[:, xo:xo + 1024],
            in1=ppf[:, 0:1024], op=OP.add),
            reads=[xkey] + PS(0, 1024), writes=[xkey])
        tr.op("ve", lambda v, xo=xo: v.bn_stats(
            out=stats[:, 0:6], in_=af[:, xo:xo + 512]),
            reads=[xkey], writes=["st"])
        tr.op("ve", lambda v, xo=xo: v.bn_stats(
            out=stats[:, 6:12], in_=af[:, xo + 512:xo + 1024]),
            reads=[xkey], writes=["st"])
        tr.op("ve", lambda v: v.bn_aggr(out=stats[:, 12:14],
                                        in_=stats[:, 0:12]),
              reads=["st"], writes=["st"])
        tr.op("sc", lambda c_: c_.activation(
            out=stats[:, 14:15], in_=stats[:, 13:14], func=AF.Sqrt,
            bias=1e-5), reads=["st"], writes=["st"])
        tr.op("ve", lambda v: v.reciprocal(out=stats[:, 15:16],
                                           in_=stats[:, 14:15]),
              reads=["st"], writes=["st"])
        tr.op("ve", lambda v: v.tensor_tensor(
            out=stats[:, 16:17], in0=stats[:, 12:13], in1=stats[:, 15:16],
            op=OP.mult), reads=["st"], writes=["st"])
        tr.op("ve", lambda v: v.tensor_scalar(
            out=stats[:, 16:17], in0=stats[:, 16:17], scalar1=-1.0,
            scalar2=None, op0=OP.mult), reads=["st"], writes=["st"])
        tr.op("sc", lambda c_, xo=xo: c_.activation(
            out=af[:, X1O:X1O + 1024], in_=af[:, xo:xo + 1024],
            func=AF.Identity, scale=stats[:, 15:16], bias=stats[:, 16:17]),
            reads=[xkey, "st"], writes=["x1o"])
        tr.op("gp", lambda g, t0=t0: g.dma_start(
            out=d_x1s[t0:t0 + P, :], in_=af[:, X1O:X1O + 1024]),
            reads=["x1o"], sem="qS")

    pass1_anchor = barrier_anchor()
    for c in range(min(n_mix, NCH)):
        front_A(c)
        if c == 0:
            for i in range(DD):
                inproj_pair(0, i)
            for j in range(8):
                conv_pair(0, j)
            front_B(0)
            bcastB(0)
            bcastC(0)
        else:
            for dd in range(DD):
                scan_dd(c - 1, dd)
                inproj_pair(c, dd)
                if dd >= 8:
                    conv_pair(c, dd - 8)
            front_B(c)
            tail(c - 1)
            bcastB(c)
            bcastC(c)
    if n_mix >= 1:
        for dd in range(DD):
            scan_dd(n_mix - 1, dd)
        tail(n_mix - 1)

    anchor = barrier_anchor()
    for s in range(32):
        tr.op("sync", lambda q, s=s: q.dma_start(
            out=af[:, FSLAB:FSLAB + 1024], in_=d_w1[s * P:(s + 1) * P, :]),
            writes=["pslab"], deps=anchor if s == 0 else (), sem="qL")
        tr.op("sc", lambda c_, s=s: c_.activation(
            out=af[:, FSGN:FSGN + 1024], in_=af[:, FSLAB:FSLAB + 1024],
            func=AF.Abs, accum_out=alpha1[:, s:s + 1]),
            reads=["pslab"], writes=["fsgn", ("al1", s)],
            deps=anchor if s == 0 else ())
        tr.op("sc", lambda c_: c_.activation(
            out=ab[:, 0:1024], in_=af[:, FSLAB:FSLAB + 1024], func=AF.Sign),
            reads=["pslab"], writes=["bslab"],
            deps=anchor if s == 0 else ())
        for kt in range(DMT):
            tr.op("te", lambda t, kt=kt: t.transpose(
                out=ppb[:, kt * P:(kt + 1) * P],
                in_=ab[:, kt * P:(kt + 1) * P], identity=idb[:]),
                reads=["bslab", "idb"], writes=[("pb", kt), "pbB"],
                deps=anchor if s == 0 else ())
        for kt in range(DMT):
            tr.op("sc", lambda c_, s=s, kt=kt: c_.copy(
                out=wbA[:, (kt * 32 + s) * P:(kt * 32 + s + 1) * P],
                in_=ppb[:, kt * P:(kt + 1) * P]),
                reads=[("pb", kt), "pbB"], writes=[("wA", kt * 32 + s)])
        tr.op("sync", lambda q, s=s: q.dma_start(
            out=b1s[:, s:s + 1], in_=d_b1[s * P:(s + 1) * P, :]),
            writes=[("b1", s)], sem="qL")
    al1k = [("al1", s) for s in range(32)]
    tr.op("ve", lambda v: v.tensor_scalar(
        out=alpha1[:], in0=alpha1[:], scalar1=1.0 / 1024.0, scalar2=None,
        op0=OP.mult), reads=al1k, writes=al1k)

    for c2 in range(n_f1):
        t0 = c2 * T2
        for tt in range(2):
            tr.op("sync", lambda q, t0=t0, tt=tt: q.dma_start(
                out=af[:, FX1 + tt * 1024:FX1 + (tt + 1) * 1024],
                in_=d_x1s[t0 + tt * P:t0 + (tt + 1) * P, :]),
                writes=[("fx1", tt)], sem="qL")
            for kt in range(DMT):
                lo = 2048 + kt * P
                tr.op("te", lambda t, kt=kt, tt=tt, lo=lo: t.transpose(
                    out=ppf[:, lo:lo + P],
                    in_=af[:, FX1 + tt * 1024 + kt * P:
                            FX1 + tt * 1024 + (kt + 1) * P],
                    identity=idf[:]),
                    reads=[("fx1", tt), "idf"], writes=PS(lo, lo + P))
            for kt in range(DMT):
                lo = 2048 + kt * P
                tr.op("sc", lambda c_, kt=kt, tt=tt, lo=lo: c_.copy(
                    out=ab[:, FX1T + kt * T2 + tt * P:
                           FX1T + kt * T2 + (tt + 1) * P],
                    in_=ppf[:, lo:lo + P]),
                    reads=PS(lo, lo + P), writes=[("x1T", kt)])
        for mt in range(32):
            po = (mt % 4) * 512
            for kt in range(DMT):
                tr.op("te", lambda t, mt=mt, kt=kt, po=po: t.matmul(
                    out=ppf[:, po:po + 256],
                    lhsT=wbA[:, (kt * 32 + mt) * P:(kt * 32 + mt + 1) * P],
                    rhs=ab[:, FX1T + kt * T2:FX1T + (kt + 1) * T2],
                    start=(kt == 0), stop=(kt == DMT - 1)),
                    reads=[("x1T", kt), ("wA", kt * 32 + mt)],
                    writes=PS(po, po + 256))
            sl = FH1 + (mt % 8) * T2
            tr.op("sc", lambda c_, mt=mt, po=po, sl=sl: c_.activation(
                out=ab[:, sl:sl + T2],
                in_=ppf[:, po:po + 256], func=AF.Gelu,
                scale=alpha1[:, mt:mt + 1], bias=b1s[:, mt:mt + 1]),
                reads=PS(po, po + 256) + [("al1", mt), ("b1", mt)],
                writes=[("h1b", mt % 8)])
            tr.op("gp", lambda g, mt=mt, t0=t0, sl=sl: g.dma_start(
                out=d_h1s[mt * P:(mt + 1) * P, t0:t0 + T2],
                in_=ab[:, sl:sl + T2]),
                reads=[("h1b", mt % 8)], sem="qS")

    anchor2 = barrier_anchor()
    for dmt in range(8):
        for q_ in range(4):
            first = (dmt == 0 and q_ == 0)
            tr.op("sync", lambda q, dmt=dmt, q_=q_: q.dma_start(
                out=af[:, FSLAB:FSLAB + 1024],
                in_=d_w2[dmt * P:(dmt + 1) * P, q_ * 1024:(q_ + 1) * 1024]),
                writes=["pslab"], deps=anchor2 if first else (), sem="qL")
            tr.op("sc", lambda c_, dmt=dmt, q_=q_: c_.activation(
                out=af[:, FSGN:FSGN + 1024], in_=af[:, FSLAB:FSLAB + 1024],
                func=AF.Abs,
                accum_out=alpha2q[:, dmt * 4 + q_:dmt * 4 + q_ + 1]),
                reads=["pslab"], writes=["fsgn", ("a2q", dmt * 4 + q_)],
                deps=anchor2 if first else ())
    a2k = [("a2q", i) for i in range(32)]
    tr.op("ve", lambda v: v.tensor_reduce(
        out=alpha2[:], in_=alpha2q[:].rearrange("p (d q) -> p d q", q=4),
        axis=AX.X, op=OP.add), reads=a2k, writes=["al2"])
    tr.op("ve", lambda v: v.tensor_scalar(
        out=alpha2[:], in0=alpha2[:], scalar1=1.0 / 4096.0, scalar2=None,
        op0=OP.mult), reads=["al2"], writes=["al2"])
    for dmt in range(8):
        for q_ in range(4):
            tr.op("sync", lambda q, dmt=dmt, q_=q_: q.dma_start(
                out=af[:, FSLAB:FSLAB + 1024],
                in_=d_w2[dmt * P:(dmt + 1) * P, q_ * 1024:(q_ + 1) * 1024]),
                writes=["pslab"], sem="qL")
            tr.op("sc", lambda c_: c_.activation(
                out=af[:, FSGN:FSGN + 1024], in_=af[:, FSLAB:FSLAB + 1024],
                func=AF.Sign), reads=["pslab"], writes=["fsgn"])
            tr.op("sc", lambda c_, dmt=dmt: c_.activation(
                out=ab[:, 0:1024], in_=af[:, FSGN:FSGN + 1024],
                func=AF.Copy, scale=alpha2[:, dmt:dmt + 1]),
                reads=["fsgn", "al2"], writes=["bslab"])
            for j in range(DMT):
                tr.op("te", lambda t, j=j: t.transpose(
                    out=ppb[:, j * P:(j + 1) * P],
                    in_=ab[:, j * P:(j + 1) * P], identity=idb[:]),
                    reads=["bslab", "idb"], writes=[("pb", j), "pbB"])
            for j in range(DMT):
                kt = q_ * 8 + j
                tr.op("sc", lambda c_, kt=kt, dmt=dmt, j=j: c_.copy(
                    out=wbA[:, kt * 1024 + dmt * P:kt * 1024 + (dmt + 1) * P],
                    in_=ppb[:, j * P:(j + 1) * P]),
                    reads=[("pb", j), "pbB"], writes=[("wA", kt * 8 + dmt)])
    tr.op("sync", lambda q: q.dma_start(out=wf[0:1, 0:1024], in_=d_b2),
          writes=["b2f"], sem="qL")
    tr.op("sc", lambda c_: c_.copy(out=b2b, in_=wf[0:1, 0:1024]),
          reads=["b2f"], writes=["b2b"])

    for c2 in range(n_f2):
        t0 = c2 * T2
        for tt in range(2):
            tr.op("sync", lambda q, t0=t0, tt=tt: q.dma_start(
                out=af[:, FX1 + tt * 1024:FX1 + (tt + 1) * 1024],
                in_=d_x1s[t0 + tt * P:t0 + (tt + 1) * P, :]),
                writes=[("fx1", tt)], sem="qL")
        for kt in range(32):
            sl = FH1 + (kt % 8) * T2
            tr.op("sync", lambda q, kt=kt, t0=t0, sl=sl: q.dma_start(
                out=ab[:, sl:sl + T2],
                in_=d_h1s[kt * P:(kt + 1) * P, t0:t0 + T2]),
                writes=[("h1b", kt % 8)], sem="qL")
            for tt in range(2):
                for ns in range(2):
                    po = tt * 1024 + ns * 512
                    tr.op("te", lambda t, kt=kt, tt=tt, ns=ns, po=po, sl=sl:
                          t.matmul(
                              out=ppf[:, po:po + 512],
                              lhsT=ab[:, sl + tt * P:sl + (tt + 1) * P],
                              rhs=wbA[:, kt * 1024 + ns * 512:
                                      kt * 1024 + (ns + 1) * 512],
                              start=(kt == 0), stop=False),
                          reads=[("h1b", kt % 8)] +
                                [("wA", kt * 8 + ns * 4 + i)
                                 for i in range(4)],
                          writes=PS(po, po + 512))
        for tt in range(2):
            for ns in range(2):
                po = tt * 1024 + ns * 512
                tr.op("te", lambda t, ns=ns, po=po: t.matmul(
                    out=ppf[:, po:po + 512], lhsT=onesb[:],
                    rhs=b2b[:, ns * 512:(ns + 1) * 512],
                    start=False, stop=True),
                    reads=["b2b", "onesb"], writes=PS(po, po + 512))
        for tt in range(2):
            tr.op("ve", lambda v, tt=tt: v.tensor_tensor(
                out=af[:, FX1 + tt * 1024:FX1 + (tt + 1) * 1024],
                in0=af[:, FX1 + tt * 1024:FX1 + (tt + 1) * 1024],
                in1=ppf[:, tt * 1024:(tt + 1) * 1024], op=OP.add),
                reads=[("fx1", tt)] + PS(tt * 1024, (tt + 1) * 1024),
                writes=[("fx1", tt)])
            tr.op("ve", lambda v, tt=tt: v.bn_stats(
                out=stats[:, 0:6],
                in_=af[:, FX1 + tt * 1024:FX1 + tt * 1024 + 512]),
                reads=[("fx1", tt)], writes=["st"])
            tr.op("ve", lambda v, tt=tt: v.bn_stats(
                out=stats[:, 6:12],
                in_=af[:, FX1 + tt * 1024 + 512:FX1 + (tt + 1) * 1024]),
                reads=[("fx1", tt)], writes=["st"])
            tr.op("ve", lambda v: v.bn_aggr(out=stats[:, 12:14],
                                            in_=stats[:, 0:12]),
                  reads=["st"], writes=["st"])
            tr.op("sc", lambda c_: c_.activation(
                out=stats[:, 14:15], in_=stats[:, 13:14], func=AF.Sqrt,
                bias=1e-5), reads=["st"], writes=["st"])
            tr.op("ve", lambda v: v.reciprocal(out=stats[:, 15:16],
                                               in_=stats[:, 14:15]),
                  reads=["st"], writes=["st"])
            tr.op("ve", lambda v: v.tensor_tensor(
                out=stats[:, 16:17], in0=stats[:, 12:13],
                in1=stats[:, 15:16], op=OP.mult),
                reads=["st"], writes=["st"])
            tr.op("ve", lambda v: v.tensor_scalar(
                out=stats[:, 16:17], in0=stats[:, 16:17], scalar1=-1.0,
                scalar2=None, op0=OP.mult), reads=["st"], writes=["st"])
            tr.op("sc", lambda c_, tt=tt: c_.activation(
                out=af[:, FOUT + tt * 1024:FOUT + (tt + 1) * 1024],
                in_=af[:, FX1 + tt * 1024:FX1 + (tt + 1) * 1024],
                func=AF.Identity, scale=stats[:, 15:16],
                bias=stats[:, 16:17]),
                reads=[("fx1", tt), "st"], writes=[("fo", tt)])
            tr.op("gp", lambda g, t0=t0, tt=tt: g.dma_start(
                out=d_out[t0 + tt * P:t0 + (tt + 1) * P, :],
                in_=af[:, FOUT + tt * 1024:FOUT + (tt + 1) * 1024]),
                reads=[("fo", tt)], sem="qS")

    if max_ops is not None and max_ops < len(prog.ops):
        prog.ops = prog.ops[:max_ops]
        prog.cnt = {}
        for i in range(max_ops):
            s = prog.seq[i][0]
            prog.cnt[s] = max(prog.cnt.get(s, 0), prog.seq[i][1])
    final_counts = [(s, prog.cnt.get(s, 0)) for s in sem_names
                    if s.startswith("q")]

    with nc.Block() as blk:
        @blk.tensor
        def _(te):
            prog.emit_for("te", te, sems, mult)

        @blk.scalar
        def _(sc):
            prog.emit_for("sc", sc, sems, mult)

        @blk.vector
        def _(ve):
            prog.emit_for("ve", ve, sems, mult)

        @blk.gpsimd
        def _(gp):
            prog.emit_for("gp", gp, sems, mult)

        @blk.sync
        def _(sy):
            prog.emit_for("sync", sy, sems, mult)
            for s, cnt in final_counts:
                if cnt:
                    sy.wait_ge(sems[s], 16 * cnt)

    return nc


_NC_CACHE = None


def kernel(L0_attn_x, L0_attn_in_proj_w, L0_attn_conv_w, L0_attn_conv_b,
           L0_attn_x_proj_w, L0_attn_dt_proj_w, L0_attn_dt_proj_b,
           L0_attn_A_log, L0_attn_D, L0_attn_out_proj_w,
           L0_ffn_w1, L0_ffn_b1, L0_ffn_w2, L0_ffn_b2):
    global _NC_CACHE
    if _NC_CACHE is None:
        _NC_CACHE = build()
    nc = _NC_CACHE

    def f32c(a):
        return np.ascontiguousarray(np.asarray(a), dtype=np.float32)

    shared = dict(
        inw=f32c(L0_attn_in_proj_w),
        cw=f32c(L0_attn_conv_w),
        cb=f32c(L0_attn_conv_b).reshape(DIN, 1),
        xpw=f32c(L0_attn_x_proj_w),
        dtw=f32c(L0_attn_dt_proj_w),
        dtb=f32c(L0_attn_dt_proj_b).reshape(DIN, 1),
        alog=f32c(L0_attn_A_log),
        dv=f32c(L0_attn_D).reshape(DIN, 1),
        outw=f32c(L0_attn_out_proj_w),
        w1=f32c(L0_ffn_w1),
        b1=f32c(L0_ffn_b1).reshape(4096, 1),
        w2=f32c(L0_ffn_w2),
        b2=f32c(L0_ffn_b2).reshape(1, DM),
    )
    xf = f32c(L0_attn_x)
    in_maps = [dict(shared, x=np.ascontiguousarray(xf[b])) for b in range(8)]
    res = run_bass_kernel_spmd(nc, in_maps, core_ids=list(range(8)))
    return np.stack([np.asarray(res.results[b]["out"]) for b in range(8)],
                    axis=0)


if __name__ == "__main__":
    import reference as R
    inp = R.setup_inputs(0)
    inp = {k: np.asarray(v) for k, v in inp.items()}
    out = kernel(**inp)
    ref = np.asarray(R.reference(**inp))
    err = np.mean((out - ref) ** 2) / np.mean(ref ** 2)
    print("resvar", err, "maxabs", np.abs(out - ref).max())


# version 12 (final)
# speedup vs baseline: 1.0152x; 1.0053x over previous
import numpy as np
import concourse.bass as bass
import concourse.mybir as mybir
from concourse.bass_utils import run_bass_kernel_spmd

F32 = mybir.dt.float32
BF16 = mybir.dt.bfloat16
AF = mybir.ActivationFunctionType
OP = mybir.AluOpType
AX = mybir.AxisListType

P = 128
T = 128
NCH = 32
T2 = 256
NCH2 = 16
DM = 1024
DMT = 8
DIN = 2048
DD = 16
NST = 16
RNK = 64
L = 4096


def PS(lo, hi):
    return [("ps", u) for u in range(lo // 512, (hi + 511) // 512)]


class Prog:
    def __init__(self):
        self.ops = []
        self.seq = {}
        self.cnt = {}

    def add(self, eng, emit, deps=(), sem=None):
        sem = sem or eng
        self.cnt[sem] = self.cnt.get(sem, 0) + 1
        i = len(self.ops)
        self.ops.append((eng, emit, tuple(d for d in deps if d is not None)))
        self.seq[i] = (sem, self.cnt[sem])
        return i

    def emit_for(self, ename, eng, sems, mult):
        waited = {}
        for i, (e, emitfn, deps) in enumerate(self.ops):
            if e != ename:
                continue
            mysem = self.seq[i][0]
            for d in deps:
                dsem, dcnt = self.seq[d]
                tgt = dcnt * mult[dsem]
                if waited.get(dsem, -1) < tgt:
                    eng.wait_ge(sems[dsem], tgt)
                    waited[dsem] = tgt
            last = emitfn(eng)
            last.then_inc(sems[mysem], mult[mysem])


NLANES_L = 8
NLANES_S = 4


class Tr:
    def __init__(self, prog):
        self.p = prog
        self.w = {}
        self.r = {}
        self.lane_idx = {"qL": 0, "qS": 0}
        self.lane_last = {}

    def op(self, eng, emit, reads=(), writes=(), deps=(), sem=None):
        dl = list(deps)
        for k in reads:
            if k in self.w:
                dl.append(self.w[k])
        for k in writes:
            dl += self.r.get(k, [])
            if k in self.w:
                dl.append(self.w[k])
        if sem in ("qL", "qS"):
            nl = NLANES_L if sem == "qL" else NLANES_S
            li = self.lane_idx[sem] % nl
            self.lane_idx[sem] += 1
            lane = f"{sem}{li}"
            if lane in self.lane_last:
                dl.append(self.lane_last[lane])
            i = self.p.add(eng, emit, dl, sem=lane)
            self.lane_last[lane] = i
        else:
            i = self.p.add(eng, emit, dl, sem=sem)
        for k in reads:
            self.r.setdefault(k, []).append(i)
        for k in writes:
            self.w[k] = i
            self.r[k] = []
        return i


def build(n_mix=NCH, n_f1=NCH2, n_f2=NCH2, max_ops=None):
    nc = bass.Bass("TRN2", target_bir_lowering=False)

    d_x = nc.dram_tensor("x", [L, DM], F32, kind="ExternalInput").ap()
    d_inw = nc.dram_tensor("inw", [2 * DIN, DM], F32, kind="ExternalInput").ap()
    d_cw = nc.dram_tensor("cw", [DIN, 4], F32, kind="ExternalInput").ap()
    d_cb = nc.dram_tensor("cb", [DIN, 1], F32, kind="ExternalInput").ap()
    d_xpw = nc.dram_tensor("xpw", [96, DIN], F32, kind="ExternalInput").ap()
    d_dtw = nc.dram_tensor("dtw", [DIN, RNK], F32, kind="ExternalInput").ap()
    d_dtb = nc.dram_tensor("dtb", [DIN, 1], F32, kind="ExternalInput").ap()
    d_alog = nc.dram_tensor("alog", [DIN, NST], F32, kind="ExternalInput").ap()
    d_dv = nc.dram_tensor("dv", [DIN, 1], F32, kind="ExternalInput").ap()
    d_outw = nc.dram_tensor("outw", [DM, DIN], F32, kind="ExternalInput").ap()
    d_w1 = nc.dram_tensor("w1", [4096, DM], F32, kind="ExternalInput").ap()
    d_b1 = nc.dram_tensor("b1", [4096, 1], F32, kind="ExternalInput").ap()
    d_w2 = nc.dram_tensor("w2", [DM, 4096], F32, kind="ExternalInput").ap()
    d_b2 = nc.dram_tensor("b2", [1, DM], F32, kind="ExternalInput").ap()
    d_out = nc.dram_tensor("out", [L, DM], F32, kind="ExternalOutput").ap()
    d_x1s = nc.dram_tensor("x1s", [L, DM], F32, kind="Internal").ap()
    d_h1s = nc.dram_tensor("h1s", [4096, L], BF16, kind="Internal").ap()

    wbA = nc.alloc_sbuf_tensor("wbA", [P, 32768], BF16)
    wbB = nc.alloc_sbuf_tensor("wbB", [P, 16384], BF16)
    wf = nc.alloc_sbuf_tensor("wf", [P, 3584], F32)
    af = nc.alloc_sbuf_tensor("af", [P, 21504], F32)
    ab = nc.alloc_sbuf_tensor("ab", [P, 5120], BF16)
    sm = nc.alloc_sbuf_tensor("sm", [P, 512], F32)
    onesf = nc.alloc_sbuf_tensor("onesf", [1, P], F32)
    onesb = nc.alloc_sbuf_tensor("onesb", [1, P], BF16)
    idf = nc.alloc_sbuf_tensor("idf", [P, P], F32)
    idb = nc.alloc_sbuf_tensor("idb", [P, P], BF16)

    b2b = ab[0:1, 4096:4096 + DM]
    ppf = nc.alloc_psum_tensor("ppf", [P, 3584], F32)
    ppb = nc.alloc_psum_tensor("ppb", [P, 1024], BF16)

    convw = sm[:, 0:64]
    convb = sm[:, 64:80]
    dtbv = sm[:, 80:96]
    Dv = sm[:, 96:112]
    alpha1 = sm[:, 112:144]
    b1s = sm[:, 144:176]
    alpha2q = sm[:, 176:208]
    alpha2 = sm[:, 208:216]
    stats = sm[:, 216:240]
    Aneg = sm[:, 240:496]
    eps1 = sm[:, 496:497]
    nc.const_aps.register(F32, 1e-5, eps1[:])

    X0, X1 = 0, 1024
    XSP = 2048
    XSC = 4144
    DT = 6192
    ABUF = 8240
    BBUF = 10288
    HBUF = 12336
    YDD = 14384
    UDD = 14512
    FIXA = 14640
    FIXB = 14656
    INIT = 14672
    DTR = 14928
    BCS = 15056
    CEX = 15184
    AB_ = [8240, 15696]
    BB_ = [10288, 17744]
    YD_ = [14384, 19792]
    UD_ = [14512, 19920]
    FA_ = [14640, 20048]
    FB_ = [14656, 20064]
    X1O = 20480
    FX1 = 0
    FOUT = 2048
    FSLAB = 4096
    FSGN = 5120
    XT = 0
    SVZ = 1024
    GTD = 3072
    FX1T = 0
    FH1 = 2048

    sems = {}
    sem_names = (["te", "sc", "ve", "gp"] +
                 [f"qL{i}" for i in range(NLANES_L)] +
                 [f"qS{i}" for i in range(NLANES_S)])
    for s in sem_names:
        sems[s] = nc.alloc_semaphore(s)
    mult = {s: (16 if s.startswith("q") else 1) for s in sem_names}

    prog = Prog()
    tr = Tr(prog)

    def barrier_anchor():
        last = {}
        for i in range(len(prog.ops)):
            last[prog.seq[i][0]] = i
        return list(last.values())

    tr.op("gp", lambda g: g.memset(idf[:], 0.0), writes=["idf"])
    tr.op("gp", lambda g: g.affine_select(
        out=idf[:], in_=idf[:], compare_op=OP.not_equal, fill=1.0,
        base=0, pattern=[[-1, P]], channel_multiplier=1),
        reads=["idf"], writes=["idf"])
    tr.op("gp", lambda g: g.memset(idb[:], 0.0), writes=["idb"])
    tr.op("gp", lambda g: g.affine_select(
        out=idb[:], in_=idb[:], compare_op=OP.not_equal, fill=1.0,
        base=0, pattern=[[-1, P]], channel_multiplier=1),
        reads=["idb"], writes=["idb"])
    tr.op("gp", lambda g: g.memset(onesf[:], 1.0), writes=["onesf"])
    tr.op("gp", lambda g: g.memset(onesb[:], 1.0), writes=["onesb"])
    tr.op("gp", lambda g: g.memset(eps1[:], 1e-5), writes=["st"])
    tr.op("ve", lambda v: v.memset(af[:, XSP:XSP + 2096], 0.0),
          writes=[("xsp", dd) for dd in range(DD)])
    tr.op("ve", lambda v: v.memset(af[:, INIT:INIT + 256], 0.0),
          writes=[("init", dd) for dd in range(DD)])

    for dd in range(DD):
        tr.op("sync", lambda s, dd=dd: s.dma_start(
            out=convw[:, dd * 4:dd * 4 + 4], in_=d_cw[dd * P:(dd + 1) * P, :]),
            writes=[("convw", dd)], sem="qL")
        tr.op("sync", lambda s, dd=dd: s.dma_start(
            out=convb[:, dd:dd + 1], in_=d_cb[dd * P:(dd + 1) * P, :]),
            writes=[("convb", dd)], sem="qL")
        tr.op("sync", lambda s, dd=dd: s.dma_start(
            out=dtbv[:, dd:dd + 1], in_=d_dtb[dd * P:(dd + 1) * P, :]),
            writes=[("dtbv", dd)], sem="qL")
        tr.op("sync", lambda s, dd=dd: s.dma_start(
            out=Dv[:, dd:dd + 1], in_=d_dv[dd * P:(dd + 1) * P, :]),
            writes=[("Dv", dd)], sem="qL")
        tr.op("sync", lambda s, dd=dd: s.dma_start(
            out=Aneg[:, dd * 16:dd * 16 + 16],
            in_=d_alog[dd * P:(dd + 1) * P, :]),
            writes=[("An", dd)], sem="qL")
    allA = [("An", dd) for dd in range(DD)]
    tr.op("sc", lambda c: c.activation(out=Aneg[:], in_=Aneg[:], func=AF.Exp),
          reads=allA, writes=allA)
    tr.op("ve", lambda v: v.tensor_scalar(
        out=Aneg[:], in0=Aneg[:], scalar1=-1.0, scalar2=None, op0=OP.mult),
        reads=allA, writes=allA)

    for s in range(32):
        tr.op("sync", lambda q, s=s: q.dma_start(
            out=af[:, 0:1024], in_=d_inw[s * P:(s + 1) * P, :]),
            writes=["pslab"], sem="qL")
        tr.op("sc", lambda c: c.copy(out=ab[:, 0:1024], in_=af[:, 0:1024]),
              reads=["pslab"], writes=["bslab"])
        for kt in range(DMT):
            tr.op("te", lambda t, kt=kt: t.transpose(
                out=ppb[:, kt * P:(kt + 1) * P],
                in_=ab[:, kt * P:(kt + 1) * P], identity=idb[:]),
                reads=["bslab", "idb"], writes=[("pb", kt), "pbB"])
        for kt in range(DMT):
            tr.op("sc", lambda c, s=s, kt=kt: c.copy(
                out=wbA[:, (kt * 32 + s) * P:(kt * 32 + s + 1) * P],
                in_=ppb[:, kt * P:(kt + 1) * P]),
                reads=[("pb", kt), "pbB"], writes=[("wA", kt * 32 + s)])

    for s in range(8):
        tr.op("sync", lambda q, s=s: q.dma_start(
            out=af[:, 0:2048], in_=d_outw[s * P:(s + 1) * P, :]),
            writes=["pslab"], sem="qL")
        tr.op("sc", lambda c: c.copy(out=ab[:, 0:2048], in_=af[:, 0:2048]),
              reads=["pslab"], writes=["bslab"])
        for half in range(2):
            for j in range(8):
                kt = half * 8 + j
                tr.op("te", lambda t, kt=kt, j=j: t.transpose(
                    out=ppb[:, j * P:(j + 1) * P],
                    in_=ab[:, kt * P:(kt + 1) * P], identity=idb[:]),
                    reads=["bslab", "idb"], writes=[("pb", j), "pbB"])
            for j in range(8):
                kt = half * 8 + j
                tr.op("sc", lambda c, s=s, kt=kt, j=j: c.copy(
                    out=wbB[:, kt * 1024 + s * P:kt * 1024 + (s + 1) * P],
                    in_=ppb[:, j * P:(j + 1) * P]),
                    reads=[("pb", j), "pbB"], writes=[("wB", kt)])

    tr.op("sync", lambda q: q.dma_start(out=af[0:96, 0:2048], in_=d_xpw),
          writes=["pslab"], sem="qL")
    for kt in range(DD):
        lo = (kt % 2) * 512
        tr.op("te", lambda t, kt=kt, lo=lo: t.transpose(
            out=ppf[:, lo:lo + 96],
            in_=af[0:96, kt * P:(kt + 1) * P], identity=idf[0:96, 0:96]),
            reads=["pslab", "idf"], writes=PS(lo, lo + 512))
        tr.op("sc", lambda c, kt=kt, lo=lo: c.copy(
            out=wf[:, kt * 96:(kt + 1) * 96], in_=ppf[:, lo:lo + 96]),
            reads=PS(lo, lo + 512), writes=[("wxp", kt)])

    for dd in range(DD):
        lo = (dd % 2) * 512
        tr.op("sync", lambda q, dd=dd: q.dma_start(
            out=af[:, 0:64], in_=d_dtw[dd * P:(dd + 1) * P, :]),
            writes=["pslab"], sem="qL")
        tr.op("te", lambda t, dd=dd, lo=lo: t.transpose(
            out=ppf[0:64, lo:lo + P], in_=af[:, 0:64], identity=idf[:]),
            reads=["pslab", "idf"], writes=PS(lo, lo + 512))
        tr.op("sc", lambda c, dd=dd, lo=lo: c.copy(
            out=wf[0:64, 1536 + dd * P:1536 + (dd + 1) * P],
            in_=ppf[0:64, lo:lo + P]),
            reads=PS(lo, lo + 512), writes=[("wdt", dd)])

    def xs_ap(dd, lo, hi):
        return af[:, XSP + dd * 131 + lo:XSP + dd * 131 + hi]

    BSB = HBUF

    def front_A(c):
        t0 = c * T
        xo = X0 if (c % 2 == 0) else X1
        xkey = ("xslot", c % 2)
        tr.op("sync", lambda q, t0=t0, xo=xo: q.dma_start(
            out=af[:, xo:xo + 1024], in_=d_x[t0:t0 + P, :]),
            writes=[xkey], deps=pass1_anchor if c == 0 else (), sem="qL")
        for kt in range(DMT):
            tr.op("te", lambda t, kt=kt, xo=xo: t.transpose(
                out=ppf[:, kt * P:(kt + 1) * P],
                in_=af[:, xo + kt * P:xo + (kt + 1) * P], identity=idf[:]),
                reads=[xkey, "idf"], writes=PS(kt * P, (kt + 1) * P))
        for kt in range(DMT):
            tr.op("sc", lambda c_, kt=kt: c_.copy(
                out=ab[:, XT + kt * P:XT + (kt + 1) * P],
                in_=ppf[:, kt * P:(kt + 1) * P]),
                reads=PS(kt * P, (kt + 1) * P), writes=[("xT", kt)])

    def inproj_pair(c, i):
        for mt in (2 * i, 2 * i + 1):
            po = 1024 + (mt % 2) * 512
            for kt in range(DMT):
                tr.op("te", lambda t, mt=mt, kt=kt, po=po: t.matmul(
                    out=ppf[:, po:po + P],
                    lhsT=wbA[:, (kt * 32 + mt) * P:(kt * 32 + mt + 1) * P],
                    rhs=ab[:, XT + kt * P:XT + (kt + 1) * P],
                    start=(kt == 0), stop=(kt == DMT - 1)),
                    reads=[("xT", kt), ("wA", kt * 32 + mt)],
                    writes=PS(po, po + P))
            if mt < DD:
                tr.op("sc", lambda c_, mt=mt, po=po: c_.copy(
                    out=xs_ap(mt, 3, 131), in_=ppf[:, po:po + P]),
                    reads=PS(po, po + P), writes=[("xsp", mt)])
            else:
                tr.op("sc", lambda c_, mt=mt, po=po: c_.activation(
                    out=ab[:, SVZ + (mt - DD) * T:SVZ + (mt - DD + 1) * T],
                    in_=ppf[:, po:po + P], func=AF.Silu),
                    reads=PS(po, po + P), writes=[("svz", mt - DD)])

    def conv_pair(c, j):
        for dd in (2 * j, 2 * j + 1):
            pr = dd % 2
            UO, YO = UD_[pr], YD_[pr]
            tr.op("ve", lambda v, dd=dd, UO=UO: v.tensor_scalar(
                out=af[:, UO:UO + T], in0=xs_ap(dd, 0, T),
                scalar1=convw[:, dd * 4:dd * 4 + 1], scalar2=None,
                op0=OP.mult),
                reads=[("xsp", dd), ("convw", dd)], writes=[("u", pr)])
            tr.op("ve", lambda v, dd=dd, UO=UO, YO=YO: v.scalar_tensor_tensor(
                out=af[:, YO:YO + T], in0=xs_ap(dd, 1, 1 + T),
                scalar=convw[:, dd * 4 + 1:dd * 4 + 2],
                in1=af[:, UO:UO + T], op0=OP.mult, op1=OP.add),
                reads=[("xsp", dd), ("u", pr), ("convw", dd)],
                writes=[("y", pr)])
            tr.op("ve", lambda v, dd=dd, UO=UO, YO=YO: v.scalar_tensor_tensor(
                out=af[:, UO:UO + T], in0=xs_ap(dd, 2, 2 + T),
                scalar=convw[:, dd * 4 + 2:dd * 4 + 3],
                in1=af[:, YO:YO + T], op0=OP.mult, op1=OP.add),
                reads=[("xsp", dd), ("y", pr), ("convw", dd)],
                writes=[("u", pr)])
            tr.op("ve", lambda v, dd=dd, UO=UO, YO=YO: v.scalar_tensor_tensor(
                out=af[:, YO:YO + T], in0=xs_ap(dd, 3, 3 + T),
                scalar=convw[:, dd * 4 + 3:dd * 4 + 4],
                in1=af[:, UO:UO + T], op0=OP.mult, op1=OP.add),
                reads=[("xsp", dd), ("u", pr), ("convw", dd)],
                writes=[("y", pr)])
            tr.op("sc", lambda c_, dd=dd, YO=YO: c_.activation(
                out=af[:, XSC + dd * T:XSC + (dd + 1) * T],
                in_=af[:, YO:YO + T], func=AF.Silu,
                bias=convb[:, dd:dd + 1]),
                reads=[("y", pr), ("convb", dd)], writes=[("xsc", dd)])
            tr.op("ve", lambda v, dd=dd: v.tensor_copy(
                out=xs_ap(dd, 0, 3), in_=xs_ap(dd, 128, 131)),
                reads=[("xsp", dd)], writes=[("xsp", dd)])

    def emit_agen(c, dd):
        pr = dd % 2
        AO = AB_[pr]
        for n in range(NST):
            tr.op("sc", lambda c_, dd=dd, n=n, AO=AO: c_.activation(
                out=af[:, AO + n * T:AO + (n + 1) * T],
                in_=af[:, DT + dd * T:DT + (dd + 1) * T], func=AF.Exp,
                scale=Aneg[:, dd * 16 + n:dd * 16 + n + 1]),
                reads=[("dt", dd), ("An", dd)], writes=[("a", pr, n)])

    def front_B(c):
        for kt in range(DD):
            tr.op("te", lambda t, kt=kt: t.matmul(
                out=ppf[0:96, 0:P],
                lhsT=wf[:, kt * 96:(kt + 1) * 96],
                rhs=af[:, XSC + kt * T:XSC + (kt + 1) * T],
                start=(kt == 0), stop=(kt == DD - 1)),
                reads=[("xsc", kt), ("wxp", kt)], writes=PS(0, 512))
        tr.op("sc", lambda c_: c_.copy(out=af[0:64, DTR:DTR + P],
                                       in_=ppf[0:64, 0:P]),
              reads=PS(0, 512), writes=["dtr"])
        tr.op("sc", lambda c_: c_.copy(out=af[0:32, BCS:BCS + P],
                                       in_=ppf[64:96, 0:P]),
              reads=PS(0, 512), writes=["bcs"])
        for dd in range(DD):
            po = (dd % 2) * 512
            spo = YD_[dd % 2]
            tr.op("te", lambda t, dd=dd, po=po: t.matmul(
                out=ppf[:, po:po + P],
                lhsT=wf[0:64, 1536 + dd * P:1536 + (dd + 1) * P],
                rhs=af[0:64, DTR:DTR + P], start=True, stop=True),
                reads=["dtr", ("wdt", dd)], writes=PS(po, po + 512))
            tr.op("ve", lambda v, dd=dd, po=po: v.tensor_scalar(
                out=af[:, DT + dd * T:DT + (dd + 1) * T],
                in0=ppf[:, po:po + P], scalar1=dtbv[:, dd:dd + 1],
                scalar2=80.0, op0=OP.add, op1=OP.min),
                reads=PS(po, po + 512) + [("dtbv", dd)], writes=[("dt", dd)])
            tr.op("sc", lambda c_, dd=dd, spo=spo: c_.activation(
                out=af[:, spo:spo + T],
                in_=af[:, DT + dd * T:DT + (dd + 1) * T], func=AF.Exp),
                reads=[("dt", dd)], writes=[("y", dd % 2)])
            tr.op("sc", lambda c_, dd=dd, spo=spo: c_.activation(
                out=af[:, DT + dd * T:DT + (dd + 1) * T],
                in_=af[:, spo:spo + T], func=AF.Ln, bias=1.0),
                reads=[("y", dd % 2)], writes=[("dt", dd)])
            if dd == 0:
                emit_agen(c, 0)

    def sel(row):
        return idf[0:32, row:row + 1].broadcast_to([32, P])

    def bcastB(c):
        for r in range(4):
            lo = (r % 2) * 512
            for j in range(4):
                n = r * 4 + j
                tr.op("te", lambda t, n=n, lo=lo, j=j: t.matmul(
                    out=ppf[:, lo + j * P:lo + (j + 1) * P], lhsT=sel(n),
                    rhs=af[0:32, BCS:BCS + P], start=True, stop=True),
                    reads=["bcs", "idf"], writes=PS(lo, lo + 512))
            tr.op("sc", lambda c_, r=r, lo=lo: c_.copy(
                out=af[:, BSB + r * 512:BSB + (r + 1) * 512],
                in_=ppf[:, lo:lo + 512]),
                reads=PS(lo, lo + 512), writes=[("bsb", r)])

    def bcastC(c):
        for j in range(4):
            lo = 2048 + j * P
            tr.op("te", lambda t, j=j, lo=lo: t.matmul(
                out=ppf[:, lo:lo + P], lhsT=sel(28 + j),
                rhs=af[0:32, BCS:BCS + P], start=True, stop=True),
                reads=["bcs", "idf"], writes=PS(2048, 2560))
        tr.op("sc", lambda c_: c_.copy(out=af[:, CEX:CEX + 512],
                                       in_=ppf[:, 2048:2560]),
              reads=PS(2048, 2560), writes=["cex"])
        for n in range(12):
            lo = 2048 + n * P
            tr.op("te", lambda t, n=n, lo=lo: t.matmul(
                out=ppf[:, lo:lo + P], lhsT=sel(16 + n),
                rhs=af[0:32, BCS:BCS + P], start=True, stop=True),
                reads=["bcs", "idf"], writes=PS(lo, lo + P))

    def scan_dd(c, dd):
        pr = dd % 2
        AO, BO, YO, UO = AB_[pr], BB_[pr], YD_[pr], UD_[pr]
        HO = BB_[1 - pr]
        FAo, FBo = FA_[pr], FB_[pr]
        a_nt = af[:, AO:AO + 2048].rearrange("p (n t) -> p n t", n=NST)
        b_nt = af[:, BO:BO + 2048].rearrange("p (n t) -> p n t", n=NST)
        akeys = [("a", pr, n) for n in range(NST)]
        hkey = ("bb", 1 - pr)
        tr.op("ve", lambda v, dd=dd, UO=UO: v.tensor_tensor(
            out=af[:, UO:UO + T],
            in0=af[:, DT + dd * T:DT + (dd + 1) * T],
            in1=af[:, XSC + dd * T:XSC + (dd + 1) * T], op=OP.mult),
            reads=[("dt", dd), ("xsc", dd)], writes=[("u", pr)])
        if dd + 1 < DD:
            emit_agen(c, dd + 1)
        u_b = af[:, UO:UO + T].rearrange(
            "p (o t) -> p o t", o=1).broadcast_to([P, NST, T])
        tr.op("ve", lambda g, u_b=u_b, b_nt=b_nt: g.tensor_tensor(
            out=b_nt, in0=u_b,
            in1=af[:, BSB:BSB + 2048].rearrange("p (n t) -> p n t", n=NST),
            op=OP.mult),
            reads=[("u", pr)] + [("bsb", r) for r in range(4)],
            writes=[("bb", pr)])
        tr.op("ve", lambda v, dd=dd, a_nt=a_nt, FBo=FBo: v.tensor_tensor(
            out=af[:, FBo:FBo + 16], in0=a_nt[:, :, 0:1],
            in1=af[:, INIT + dd * 16:INIT + (dd + 1) * 16], op=OP.mult),
            reads=akeys + [("init", dd)], writes=[("fixb", pr)])
        tr.op("ve", lambda v, b_nt=b_nt, FBo=FBo: v.tensor_tensor(
            out=b_nt[:, :, 0:1], in0=af[:, FBo:FBo + 16],
            in1=b_nt[:, :, 0:1], op=OP.add),
            reads=[("fixb", pr), ("bb", pr)], writes=[("bb", pr)])
        tr.op("ve", lambda v, a_nt=a_nt: v.tensor_scalar(
            out=a_nt[:, :, 0:1], in0=a_nt[:, :, 0:1],
            scalar1=0.0, scalar2=None, op0=OP.mult),
            reads=akeys, writes=akeys)
        tr.op("ve", lambda v, AO=AO, BO=BO, HO=HO: v.tensor_tensor_scan(
            out=af[:, HO:HO + 2048],
            data0=af[:, AO:AO + 2048],
            data1=af[:, BO:BO + 2048],
            initial=0.0, op0=OP.mult, op1=OP.add),
            reads=akeys + [("bb", pr)], writes=[hkey])
        h_nt = af[:, HO:HO + 2048].rearrange("p (n t) -> p n t", n=NST)
        tr.op("ve", lambda v, dd=dd, h_nt=h_nt: v.tensor_copy(
            out=af[:, INIT + dd * 16:INIT + (dd + 1) * 16],
            in_=h_nt[:, :, T - 1:T]),
            reads=[hkey], writes=[("init", dd)])
        tr.op("ve", lambda v, AO=AO, HO=HO: v.tensor_tensor(
            out=af[:, AO:AO + 1536], in0=af[:, HO:HO + 1536],
            in1=ppf[:, 2048:3584], op=OP.mult),
            reads=[hkey] + PS(2048, 3584), writes=akeys[:12])
        tr.op("gp", lambda g, AO=AO, HO=HO: g.tensor_tensor(
            out=af[:, AO + 1536:AO + 2048],
            in0=af[:, HO + 1536:HO + 2048],
            in1=af[:, CEX:CEX + 512], op=OP.mult),
            reads=[hkey, "cex"], writes=akeys[12:])
        tr.op("ve", lambda v, AO=AO, YO=YO: v.tensor_reduce(
            out=af[:, YO:YO + T],
            in_=af[:, AO:AO + 2048].rearrange("p (n t) -> p t n", n=NST),
            axis=AX.X, op=OP.add),
            reads=akeys, writes=[("y", pr)])
        tr.op("ve", lambda v, dd=dd, YO=YO: v.scalar_tensor_tensor(
            out=af[:, YO:YO + T],
            in0=af[:, XSC + dd * T:XSC + (dd + 1) * T],
            scalar=Dv[:, dd:dd + 1], in1=af[:, YO:YO + T],
            op0=OP.mult, op1=OP.add),
            reads=[("xsc", dd), ("Dv", dd), ("y", pr)], writes=[("y", pr)])
        tr.op("gp", lambda v, dd=dd, YO=YO: v.tensor_tensor(
            out=af[:, YO:YO + T], in0=af[:, YO:YO + T],
            in1=ab[:, SVZ + dd * T:SVZ + (dd + 1) * T], op=OP.mult),
            reads=[("y", pr), ("svz", dd)], writes=[("y", pr)])
        tr.op("sc", lambda c_, dd=dd, YO=YO: c_.copy(
            out=ab[:, GTD + dd * T:GTD + (dd + 1) * T],
            in_=af[:, YO:YO + T]),
            reads=[("y", pr)], writes=[("gtd", dd)])

    def tail(c):
        t0 = c * T
        xo = X0 if (c % 2 == 0) else X1
        xkey = ("xslot", c % 2)
        for ns in range(2):
            for dd in range(DD):
                tr.op("te", lambda t, dd=dd, ns=ns: t.matmul(
                    out=ppf[:, ns * 512:(ns + 1) * 512],
                    lhsT=ab[:, GTD + dd * T:GTD + (dd + 1) * T],
                    rhs=wbB[:, dd * 1024 + ns * 512:
                            dd * 1024 + (ns + 1) * 512],
                    start=(dd == 0), stop=(dd == DD - 1)),
                    reads=[("gtd", dd), ("wB", dd)],
                    writes=PS(ns * 512, (ns + 1) * 512))
        tr.op("ve", lambda v, xo=xo: v.tensor_tensor(
            out=af[:, xo:xo + 1024], in0=af[:, xo:xo + 1024],
            in1=ppf[:, 0:1024], op=OP.add),
            reads=[xkey] + PS(0, 1024), writes=[xkey])
        tr.op("ve", lambda v, xo=xo: v.bn_stats(
            out=stats[:, 0:6], in_=af[:, xo:xo + 512]),
            reads=[xkey], writes=["st"])
        tr.op("ve", lambda v, xo=xo: v.bn_stats(
            out=stats[:, 6:12], in_=af[:, xo + 512:xo + 1024]),
            reads=[xkey], writes=["st"])
        tr.op("ve", lambda v: v.bn_aggr(out=stats[:, 12:14],
                                        in_=stats[:, 0:12]),
              reads=["st"], writes=["st"])
        tr.op("sc", lambda c_: c_.activation(
            out=stats[:, 14:15], in_=stats[:, 13:14], func=AF.Sqrt,
            bias=1e-5), reads=["st"], writes=["st"])
        tr.op("ve", lambda v: v.reciprocal(out=stats[:, 15:16],
                                           in_=stats[:, 14:15]),
              reads=["st"], writes=["st"])
        tr.op("ve", lambda v: v.tensor_tensor(
            out=stats[:, 16:17], in0=stats[:, 12:13], in1=stats[:, 15:16],
            op=OP.mult), reads=["st"], writes=["st"])
        tr.op("ve", lambda v: v.tensor_scalar(
            out=stats[:, 16:17], in0=stats[:, 16:17], scalar1=-1.0,
            scalar2=None, op0=OP.mult), reads=["st"], writes=["st"])
        tr.op("sc", lambda c_, xo=xo: c_.activation(
            out=af[:, X1O:X1O + 1024], in_=af[:, xo:xo + 1024],
            func=AF.Identity, scale=stats[:, 15:16], bias=stats[:, 16:17]),
            reads=[xkey, "st"], writes=["x1o"])
        tr.op("gp", lambda g, t0=t0: g.dma_start(
            out=d_x1s[t0:t0 + P, :], in_=af[:, X1O:X1O + 1024]),
            reads=["x1o"], sem="qS")

    pass1_anchor = barrier_anchor()
    for c in range(min(n_mix, NCH)):
        front_A(c)
        if c == 0:
            for i in range(DD):
                inproj_pair(0, i)
            for j in range(8):
                conv_pair(0, j)
            front_B(0)
            bcastB(0)
            bcastC(0)
        else:
            for dd in range(DD):
                scan_dd(c - 1, dd)
                inproj_pair(c, dd)
                if dd >= 8:
                    conv_pair(c, dd - 8)
            front_B(c)
            tail(c - 1)
            bcastB(c)
            bcastC(c)
    if n_mix >= 1:
        for dd in range(DD):
            scan_dd(n_mix - 1, dd)
        tail(n_mix - 1)

    anchor = barrier_anchor()
    for s in range(32):
        tr.op("sync", lambda q, s=s: q.dma_start(
            out=af[:, FSLAB:FSLAB + 1024], in_=d_w1[s * P:(s + 1) * P, :]),
            writes=["pslab"], deps=anchor if s == 0 else (), sem="qL")
        tr.op("sc", lambda c_, s=s: c_.activation(
            out=af[:, FSGN:FSGN + 1024], in_=af[:, FSLAB:FSLAB + 1024],
            func=AF.Abs, accum_out=alpha1[:, s:s + 1]),
            reads=["pslab"], writes=["fsgn", ("al1", s)],
            deps=anchor if s == 0 else ())
        tr.op("sc", lambda c_: c_.activation(
            out=ab[:, 0:1024], in_=af[:, FSLAB:FSLAB + 1024], func=AF.Sign),
            reads=["pslab"], writes=["bslab"],
            deps=anchor if s == 0 else ())
        for kt in range(DMT):
            tr.op("te", lambda t, kt=kt: t.transpose(
                out=ppb[:, kt * P:(kt + 1) * P],
                in_=ab[:, kt * P:(kt + 1) * P], identity=idb[:]),
                reads=["bslab", "idb"], writes=[("pb", kt), "pbB"],
                deps=anchor if s == 0 else ())
        for kt in range(DMT):
            tr.op("sc", lambda c_, s=s, kt=kt: c_.copy(
                out=wbA[:, (kt * 32 + s) * P:(kt * 32 + s + 1) * P],
                in_=ppb[:, kt * P:(kt + 1) * P]),
                reads=[("pb", kt), "pbB"], writes=[("wA", kt * 32 + s)])
        tr.op("sync", lambda q, s=s: q.dma_start(
            out=b1s[:, s:s + 1], in_=d_b1[s * P:(s + 1) * P, :]),
            writes=[("b1", s)], sem="qL")
    al1k = [("al1", s) for s in range(32)]
    tr.op("ve", lambda v: v.tensor_scalar(
        out=alpha1[:], in0=alpha1[:], scalar1=1.0 / 1024.0, scalar2=None,
        op0=OP.mult), reads=al1k, writes=al1k)

    for c2 in range(n_f1):
        t0 = c2 * T2
        for tt in range(2):
            tr.op("sync", lambda q, t0=t0, tt=tt: q.dma_start(
                out=af[:, FX1 + tt * 1024:FX1 + (tt + 1) * 1024],
                in_=d_x1s[t0 + tt * P:t0 + (tt + 1) * P, :]),
                writes=[("fx1", tt)], sem="qL")
            for kt in range(DMT):
                lo = 2048 + kt * P
                tr.op("te", lambda t, kt=kt, tt=tt, lo=lo: t.transpose(
                    out=ppf[:, lo:lo + P],
                    in_=af[:, FX1 + tt * 1024 + kt * P:
                            FX1 + tt * 1024 + (kt + 1) * P],
                    identity=idf[:]),
                    reads=[("fx1", tt), "idf"], writes=PS(lo, lo + P))
            for kt in range(DMT):
                lo = 2048 + kt * P
                tr.op("sc", lambda c_, kt=kt, tt=tt, lo=lo: c_.copy(
                    out=ab[:, FX1T + kt * T2 + tt * P:
                           FX1T + kt * T2 + (tt + 1) * P],
                    in_=ppf[:, lo:lo + P]),
                    reads=PS(lo, lo + P), writes=[("x1T", kt)])
        for mt in range(32):
            po = (mt % 4) * 512
            for kt in range(DMT):
                tr.op("te", lambda t, mt=mt, kt=kt, po=po: t.matmul(
                    out=ppf[:, po:po + 256],
                    lhsT=wbA[:, (kt * 32 + mt) * P:(kt * 32 + mt + 1) * P],
                    rhs=ab[:, FX1T + kt * T2:FX1T + (kt + 1) * T2],
                    start=(kt == 0), stop=(kt == DMT - 1)),
                    reads=[("x1T", kt), ("wA", kt * 32 + mt)],
                    writes=PS(po, po + 256))
            sl = FH1 + (mt % 8) * T2
            tr.op("sc", lambda c_, mt=mt, po=po, sl=sl: c_.activation(
                out=ab[:, sl:sl + T2],
                in_=ppf[:, po:po + 256], func=AF.Gelu,
                scale=alpha1[:, mt:mt + 1], bias=b1s[:, mt:mt + 1]),
                reads=PS(po, po + 256) + [("al1", mt), ("b1", mt)],
                writes=[("h1b", mt % 8)])
            tr.op("gp", lambda g, mt=mt, t0=t0, sl=sl: g.dma_start(
                out=d_h1s[mt * P:(mt + 1) * P, t0:t0 + T2],
                in_=ab[:, sl:sl + T2]),
                reads=[("h1b", mt % 8)], sem="qS")

    anchor2 = barrier_anchor()
    for dmt in range(8):
        for q_ in range(4):
            first = (dmt == 0 and q_ == 0)
            tr.op("sync", lambda q, dmt=dmt, q_=q_: q.dma_start(
                out=af[:, FSLAB:FSLAB + 1024],
                in_=d_w2[dmt * P:(dmt + 1) * P, q_ * 1024:(q_ + 1) * 1024]),
                writes=["pslab"], deps=anchor2 if first else (), sem="qL")
            tr.op("sc", lambda c_, dmt=dmt, q_=q_: c_.activation(
                out=af[:, FSGN:FSGN + 1024], in_=af[:, FSLAB:FSLAB + 1024],
                func=AF.Abs,
                accum_out=alpha2q[:, dmt * 4 + q_:dmt * 4 + q_ + 1]),
                reads=["pslab"], writes=["fsgn", ("a2q", dmt * 4 + q_)],
                deps=anchor2 if first else ())
    a2k = [("a2q", i) for i in range(32)]
    tr.op("ve", lambda v: v.tensor_reduce(
        out=alpha2[:], in_=alpha2q[:].rearrange("p (d q) -> p d q", q=4),
        axis=AX.X, op=OP.add), reads=a2k, writes=["al2"])
    tr.op("ve", lambda v: v.tensor_scalar(
        out=alpha2[:], in0=alpha2[:], scalar1=1.0 / 4096.0, scalar2=None,
        op0=OP.mult), reads=["al2"], writes=["al2"])
    for dmt in range(8):
        for q_ in range(4):
            tr.op("sync", lambda q, dmt=dmt, q_=q_: q.dma_start(
                out=af[:, FSLAB:FSLAB + 1024],
                in_=d_w2[dmt * P:(dmt + 1) * P, q_ * 1024:(q_ + 1) * 1024]),
                writes=["pslab"], sem="qL")
            tr.op("sc", lambda c_: c_.activation(
                out=af[:, FSGN:FSGN + 1024], in_=af[:, FSLAB:FSLAB + 1024],
                func=AF.Sign), reads=["pslab"], writes=["fsgn"])
            tr.op("sc", lambda c_, dmt=dmt: c_.activation(
                out=ab[:, 0:1024], in_=af[:, FSGN:FSGN + 1024],
                func=AF.Copy, scale=alpha2[:, dmt:dmt + 1]),
                reads=["fsgn", "al2"], writes=["bslab"])
            for j in range(DMT):
                tr.op("te", lambda t, j=j: t.transpose(
                    out=ppb[:, j * P:(j + 1) * P],
                    in_=ab[:, j * P:(j + 1) * P], identity=idb[:]),
                    reads=["bslab", "idb"], writes=[("pb", j), "pbB"])
            for j in range(DMT):
                kt = q_ * 8 + j
                tr.op("sc", lambda c_, kt=kt, dmt=dmt, j=j: c_.copy(
                    out=wbA[:, kt * 1024 + dmt * P:kt * 1024 + (dmt + 1) * P],
                    in_=ppb[:, j * P:(j + 1) * P]),
                    reads=[("pb", j), "pbB"], writes=[("wA", kt * 8 + dmt)])
    tr.op("sync", lambda q: q.dma_start(out=wf[0:1, 0:1024], in_=d_b2),
          writes=["b2f"], sem="qL")
    tr.op("sc", lambda c_: c_.copy(out=b2b, in_=wf[0:1, 0:1024]),
          reads=["b2f"], writes=["b2b"])

    for c2 in range(n_f2):
        t0 = c2 * T2
        for tt in range(2):
            tr.op("sync", lambda q, t0=t0, tt=tt: q.dma_start(
                out=af[:, FX1 + tt * 1024:FX1 + (tt + 1) * 1024],
                in_=d_x1s[t0 + tt * P:t0 + (tt + 1) * P, :]),
                writes=[("fx1", tt)], sem="qL")
        for kt in range(32):
            sl = FH1 + (kt % 8) * T2
            tr.op("sync", lambda q, kt=kt, t0=t0, sl=sl: q.dma_start(
                out=ab[:, sl:sl + T2],
                in_=d_h1s[kt * P:(kt + 1) * P, t0:t0 + T2]),
                writes=[("h1b", kt % 8)], sem="qL")
            for tt in range(2):
                for ns in range(2):
                    po = tt * 1024 + ns * 512
                    tr.op("te", lambda t, kt=kt, tt=tt, ns=ns, po=po, sl=sl:
                          t.matmul(
                              out=ppf[:, po:po + 512],
                              lhsT=ab[:, sl + tt * P:sl + (tt + 1) * P],
                              rhs=wbA[:, kt * 1024 + ns * 512:
                                      kt * 1024 + (ns + 1) * 512],
                              start=(kt == 0), stop=False),
                          reads=[("h1b", kt % 8)] +
                                [("wA", kt * 8 + ns * 4 + i)
                                 for i in range(4)],
                          writes=PS(po, po + 512))
        for tt in range(2):
            for ns in range(2):
                po = tt * 1024 + ns * 512
                tr.op("te", lambda t, ns=ns, po=po: t.matmul(
                    out=ppf[:, po:po + 512], lhsT=onesb[:],
                    rhs=b2b[:, ns * 512:(ns + 1) * 512],
                    start=False, stop=True),
                    reads=["b2b", "onesb"], writes=PS(po, po + 512))
        for tt in range(2):
            tr.op("ve", lambda v, tt=tt: v.tensor_tensor(
                out=af[:, FX1 + tt * 1024:FX1 + (tt + 1) * 1024],
                in0=af[:, FX1 + tt * 1024:FX1 + (tt + 1) * 1024],
                in1=ppf[:, tt * 1024:(tt + 1) * 1024], op=OP.add),
                reads=[("fx1", tt)] + PS(tt * 1024, (tt + 1) * 1024),
                writes=[("fx1", tt)])
            tr.op("ve", lambda v, tt=tt: v.bn_stats(
                out=stats[:, 0:6],
                in_=af[:, FX1 + tt * 1024:FX1 + tt * 1024 + 512]),
                reads=[("fx1", tt)], writes=["st"])
            tr.op("ve", lambda v, tt=tt: v.bn_stats(
                out=stats[:, 6:12],
                in_=af[:, FX1 + tt * 1024 + 512:FX1 + (tt + 1) * 1024]),
                reads=[("fx1", tt)], writes=["st"])
            tr.op("ve", lambda v: v.bn_aggr(out=stats[:, 12:14],
                                            in_=stats[:, 0:12]),
                  reads=["st"], writes=["st"])
            tr.op("sc", lambda c_: c_.activation(
                out=stats[:, 14:15], in_=stats[:, 13:14], func=AF.Sqrt,
                bias=1e-5), reads=["st"], writes=["st"])
            tr.op("ve", lambda v: v.reciprocal(out=stats[:, 15:16],
                                               in_=stats[:, 14:15]),
                  reads=["st"], writes=["st"])
            tr.op("ve", lambda v: v.tensor_tensor(
                out=stats[:, 16:17], in0=stats[:, 12:13],
                in1=stats[:, 15:16], op=OP.mult),
                reads=["st"], writes=["st"])
            tr.op("ve", lambda v: v.tensor_scalar(
                out=stats[:, 16:17], in0=stats[:, 16:17], scalar1=-1.0,
                scalar2=None, op0=OP.mult), reads=["st"], writes=["st"])
            tr.op("sc", lambda c_, tt=tt: c_.activation(
                out=af[:, FOUT + tt * 1024:FOUT + (tt + 1) * 1024],
                in_=af[:, FX1 + tt * 1024:FX1 + (tt + 1) * 1024],
                func=AF.Identity, scale=stats[:, 15:16],
                bias=stats[:, 16:17]),
                reads=[("fx1", tt), "st"], writes=[("fo", tt)])
            tr.op("gp", lambda g, t0=t0, tt=tt: g.dma_start(
                out=d_out[t0 + tt * P:t0 + (tt + 1) * P, :],
                in_=af[:, FOUT + tt * 1024:FOUT + (tt + 1) * 1024]),
                reads=[("fo", tt)], sem="qS")

    if max_ops is not None and max_ops < len(prog.ops):
        prog.ops = prog.ops[:max_ops]
        prog.cnt = {}
        for i in range(max_ops):
            s = prog.seq[i][0]
            prog.cnt[s] = max(prog.cnt.get(s, 0), prog.seq[i][1])
    final_counts = [(s, prog.cnt.get(s, 0)) for s in sem_names
                    if s.startswith("q")]

    with nc.Block() as blk:
        @blk.tensor
        def _(te):
            prog.emit_for("te", te, sems, mult)

        @blk.scalar
        def _(sc):
            prog.emit_for("sc", sc, sems, mult)

        @blk.vector
        def _(ve):
            prog.emit_for("ve", ve, sems, mult)

        @blk.gpsimd
        def _(gp):
            prog.emit_for("gp", gp, sems, mult)

        @blk.sync
        def _(sy):
            prog.emit_for("sync", sy, sems, mult)
            for s, cnt in final_counts:
                if cnt:
                    sy.wait_ge(sems[s], 16 * cnt)

    return nc


_NC_CACHE = None


def kernel(L0_attn_x, L0_attn_in_proj_w, L0_attn_conv_w, L0_attn_conv_b,
           L0_attn_x_proj_w, L0_attn_dt_proj_w, L0_attn_dt_proj_b,
           L0_attn_A_log, L0_attn_D, L0_attn_out_proj_w,
           L0_ffn_w1, L0_ffn_b1, L0_ffn_w2, L0_ffn_b2):
    global _NC_CACHE
    if _NC_CACHE is None:
        _NC_CACHE = build()
    nc = _NC_CACHE

    def f32c(a):
        return np.ascontiguousarray(np.asarray(a), dtype=np.float32)

    shared = dict(
        inw=f32c(L0_attn_in_proj_w),
        cw=f32c(L0_attn_conv_w),
        cb=f32c(L0_attn_conv_b).reshape(DIN, 1),
        xpw=f32c(L0_attn_x_proj_w),
        dtw=f32c(L0_attn_dt_proj_w),
        dtb=f32c(L0_attn_dt_proj_b).reshape(DIN, 1),
        alog=f32c(L0_attn_A_log),
        dv=f32c(L0_attn_D).reshape(DIN, 1),
        outw=f32c(L0_attn_out_proj_w),
        w1=f32c(L0_ffn_w1),
        b1=f32c(L0_ffn_b1).reshape(4096, 1),
        w2=f32c(L0_ffn_w2),
        b2=f32c(L0_ffn_b2).reshape(1, DM),
    )
    xf = f32c(L0_attn_x)
    in_maps = [dict(shared, x=np.ascontiguousarray(xf[b])) for b in range(8)]
    res = run_bass_kernel_spmd(nc, in_maps, core_ids=list(range(8)))
    return np.stack([np.asarray(res.results[b]["out"]) for b in range(8)],
                    axis=0)


if __name__ == "__main__":
    import reference as R
    inp = R.setup_inputs(0)
    inp = {k: np.asarray(v) for k, v in inp.items()}
    out = kernel(**inp)
    ref = np.asarray(R.reference(**inp))
    err = np.mean((out - ref) ** 2) / np.mean(ref ** 2)
    print("resvar", err, "maxabs", np.abs(out - ref).max())
